# version 11
# baseline (speedup 1.0000x reference)
import numpy as np
import ml_dtypes
import concourse.bass as bass
import concourse.mybir as mybir
from concourse.bass_utils import run_bass_kernel_spmd

F32 = mybir.dt.float32
BF16 = mybir.dt.bfloat16
AF = mybir.ActivationFunctionType
ALU = mybir.AluOpType
AX = mybir.AxisListType
f32 = np.float32

NCORES = 8
D = 2048
SEQ = 2048
T = 4096
H = 16
NQKV = 6144
NLR = 48
INTER = 5632
NI = INTER // 128
NTC = T // 128
NS_COEFS = [(4.0848, -6.8946, 2.927), (3.9505, -6.3029, 2.6377),
            (3.7418, -5.5913, 2.3037), (2.8769, -3.1427, 1.2046),
            (2.8366, -3.0525, 1.2012)]
BASE_LR_INV = 0.01 + float(np.log(-np.expm1(-0.01)))

STAGE = 6
DEBUG_TAPS = False
P4_HM = 2 * H
P4_SECT = 9

ENGS = ("sync", "gpsimd", "tensor", "vector", "scalar")


NLANES = 10


class Buf:
    __slots__ = ("ws", "r", "excl")
    def __init__(self, fence=(), excl=False):
        self.ws = []
        self.r = list(fence)
        self.excl = excl


class Sched:
    def __init__(self, nc):
        self.nc = nc
        self.ops = []
        self.dma_cnt = {}

    def cut(self):
        last = {}
        for op in self.ops:
            if op["kind"] == "d":
                last[(op["eng"], "d", op["lane"])] = op["i"]
            else:
                last[(op["eng"], "c")] = op["i"]
        return sorted(last.values())

    def add(self, eng, kind, fn, reads=(), writes=(), deps=()):
        i = len(self.ops)
        dep = set()
        waw = set()
        for d in deps:
            if d is not None:
                assert d < i
                dep.add(d)
        writes = list(writes) + [b for b in reads if b.excl]
        reads = [b for b in reads if not b.excl]
        for b in reads:
            dep.update(b.ws)
        for b in writes:
            waw.update(b.ws)
            dep.update(b.r)
        dep -= waw
        op = dict(i=i, eng=eng, kind=kind, fn=fn,
                  deps=sorted(dep), waw=sorted(waw),
                  inc=False, idx=None, lane=None, lidx=None)
        if kind == "d":
            c = self.dma_cnt.get(eng, 0)
            self.dma_cnt[eng] = c + 1
            op["lane"] = c % NLANES
            op["lidx"] = c // NLANES + 1
        self.ops.append(op)
        for b in reads:
            b.r.append(i)
        for b in writes:
            b.ws.append(i)
        return i

    def c(self, eng, fn, reads=(), writes=(), deps=()):
        return self.add(eng, "c", fn, reads, writes, deps)

    def d(self, q, fn, reads=(), writes=(), deps=()):
        return self.add(q, "d", fn, reads, writes, deps)

    @staticmethod
    def _skip(p, op, is_waw):
        if p["eng"] != op["eng"]:
            return False
        if p["kind"] == "c" and op["kind"] == "c":
            return p["eng"] == "tensor"
        if is_waw and p["kind"] == "d" and op["kind"] == "d":
            return True
        return False

    @staticmethod
    def _key(p):
        if p["kind"] == "d":
            return (p["eng"], "d", p["lane"])
        return (p["eng"], "c")

    def emit(self, final_deps):
        nc = self.nc
        self.c("sync", lambda e: e.nop(), deps=final_deps)
        for op in self.ops:
            if op["kind"] == "d":
                op["inc"] = True
            for d in op["deps"]:
                p = self.ops[d]
                if not self._skip(p, op, False):
                    p["inc"] = True
            for d in op["waw"]:
                p = self.ops[d]
                if not self._skip(p, op, True):
                    p["inc"] = True
        counters = {}
        for op in self.ops:
            if op["kind"] == "d":
                op["idx"] = op["lidx"]
            elif op["inc"]:
                key = self._key(op)
                counters[key] = counters.get(key, 0) + 1
                op["idx"] = counters[key]
        keys = set(counters)
        for op in self.ops:
            if op["kind"] == "d":
                keys.add(self._key(op))
        sem_of = {key: nc.alloc_semaphore("s_" + "_".join(map(str, key)))
                  for key in sorted(keys)}
        for op in self.ops:
            waits = {}
            for dlist, is_waw in ((op["deps"], False), (op["waw"], True)):
                for d in dlist:
                    p = self.ops[d]
                    if self._skip(p, op, is_waw):
                        continue
                    key = self._key(p)
                    amt = 16 if p["kind"] == "d" else 1
                    v = p["idx"] * amt
                    if waits.get(key, 0) < v:
                        waits[key] = v
            if op["kind"] == "d" and op["lidx"] > 1:
                key = self._key(op)
                v = (op["lidx"] - 1) * 16
                if waits.get(key, 0) < v:
                    waits[key] = v
            op["waits"] = waits
        per_eng = {e: [] for e in ENGS}
        for op in self.ops:
            per_eng[op["eng"]].append(op)

        with nc.Block() as blk:
            for eng_name in ENGS:
                ops_list = per_eng[eng_name]
                if not ops_list:
                    continue

                def body(e, ops_list=ops_list):
                    hw = {}
                    for op in ops_list:
                        for key in sorted(op["waits"]):
                            v = op["waits"][key]
                            if hw.get(key, 0) >= v:
                                continue
                            e.wait_ge(sem_of[key], v)
                            hw[key] = v
                        inst = op["fn"](e)
                        if op["kind"] == "d":
                            inst.then_inc(sem_of[self._key(op)], 16)
                        elif op["inc"]:
                            inst.then_inc(sem_of[self._key(op)], 1)

                getattr(blk, eng_name)(body)


class Arena:
    def __init__(self, sched, tensor, elems):
        self.S = sched
        self.t = tensor
        self.elems = elems
        self.cur = 0
        self.fence = ()

    def take(self, n):
        assert self.cur + n <= self.elems, (self.cur, n, self.elems)
        off = self.cur
        self.cur += n
        return self.t[:, off:off + n]

    def buf(self):
        return Buf(self.fence)

    def reset(self):
        self.cur = 0
        self.fence = tuple(self.S.cut())


def build_nc():
    nc = bass.Bass("TRN2", target_bir_lowering=False)

    def I(name, shape):
        return nc.dram_tensor(name, list(shape), F32, kind="ExternalInput").ap()

    x_tok = I("x_tok", (T, D))
    x_feat = I("x_feat", (D, T))
    wqkv_t = I("wqkv_t", (D, NQKV + NLR))
    wg_t = I("wg_t", (D, INTER))
    wu_t = I("wu_t", (D, INTER))
    wd_t = I("wd_t", (INTER, D))
    wo_t = I("wo_t", (D, D))
    w_nat = I("w_nat", (128, 3 * H * 128))
    w02t = I("w02t", (128, H * 256))
    gate_s_r = I("gate_s_r", (128, NI))
    up_s_r = I("up_s_r", (128, NI))
    down_s_r = I("down_s_r", (128, 16))
    onorm_r = I("onorm_r", (1, 128))

    out_feat = nc.dram_tensor("out_feat", [D, T], F32, kind="ExternalOutput").ap()

    sk = "ExternalOutput" if DEBUG_TAPS else "Internal"
    xn_h = nc.dram_tensor("xn_h", [16, T, 128], BF16, kind=sk).ap()
    qT_h = nc.dram_tensor("qT_h", [2 * H, SEQ, 128], BF16, kind=sk).ap()
    kT_h = nc.dram_tensor("kT_h", [2 * H, SEQ, 128], BF16, kind=sk).ap()
    vT_h = nc.dram_tensor("vT_h", [2 * H, SEQ, 128], BF16, kind=sk).ap()
    hdn_h = nc.dram_tensor("hdn_h", [INTER, T], BF16, kind=sk).ap()
    wd_bf_h = nc.dram_tensor("wd_bf_h", [INTER, D], BF16, kind="Internal").ap()
    onrm_h = nc.dram_tensor("onrm_h", [2 * H, SEQ, 128], BF16, kind=sk).ap()
    part_h = nc.dram_tensor("part_h", [D, T], F32, kind=sk).ap()

    qT4 = qT_h.rearrange("g (u p) c -> p g u c", p=128)
    kT4 = kT_h.rearrange("g (u p) c -> p g u c", p=128)
    vT4 = vT_h.rearrange("g (u p) c -> p g u c", p=128)
    onrm4 = onrm_h.rearrange("g (u p) c -> p g u c", p=128)
    xn_tok3 = xn_h.rearrange("a (u p) c -> p u a c", p=128)
    hdn3 = hdn_h.rearrange("(a p) t -> p a t", p=128)
    wd3 = wd_t.rearrange("(a p) c -> p a c", p=128)
    wdbf3 = wd_bf_h.rearrange("(a p) c -> p a c", p=128)
    wqkv3 = wqkv_t.rearrange("(a p) o -> p a o", p=128)
    wg3 = wg_t.rearrange("(a p) o -> p a o", p=128)
    wu3 = wu_t.rearrange("(a p) o -> p a o", p=128)
    wo3 = wo_t.rearrange("(a p) o -> p a o", p=128)
    wnat3 = w_nat.rearrange("p (j h c) -> p j h c", j=3, h=H)
    w02t3 = w02t.rearrange("p (h c) -> p h c", h=H)

    A = nc.alloc_sbuf_tensor
    ident = A("ident", [128, 128], BF16)
    gam = A("gam", [128, 128], BF16)
    aI = A("aI", [128, 5 * 128], BF16)
    onesrow = A("onesrow", [1, 128], BF16)
    onescol = A("onescol", [128, 1], BF16)
    onorm_bf = A("onorm_bf", [1, 128], BF16)
    gate_s_sb = A("gate_s_sb", [128, NI], F32)
    up_s_sb = A("up_s_sb", [128, NI], F32)
    down_s_sb = A("down_s_sb", [128, 16], F32)
    lr_sb = A("lr_sb", [128, NTC * NLR], F32)
    lr3 = lr_sb[:, :].rearrange("p (u c) -> p u c", c=NLR)
    ssq_q = A("ssq_q", [128, NTC * H], F32)
    ssq_k = A("ssq_k", [128, NTC * H], F32)
    rq_sb = A("rq_sb", [128, NTC * H], F32)
    rk_sb = A("rk_sb", [128, NTC * H], F32)
    ssq_q3 = ssq_q[:, :].rearrange("p (u h) -> p u h", h=H)
    ssq_k3 = ssq_k[:, :].rearrange("p (u h) -> p u h", h=H)
    rq3 = rq_sb[:, :].rearrange("p (u h) -> p u h", h=H)
    rk3 = rk_sb[:, :].rearrange("p (u h) -> p u h", h=H)
    w02t_sb = A("w02t_sb", [128, H * 256], BF16)
    w02t_sb3 = w02t_sb[:, :].rearrange("p (h c) -> p h c", h=H)
    w1n_sb = A("w1n_sb", [128, H * 128], BF16)
    w1n_sb3 = w1n_sb[:, :].rearrange("p (h c) -> p h c", h=H)
    wnorm_sb = A("wnorm_sb", [128, 3 * H], F32)

    AR1_E = 79232
    AR2_E = 6 * 1024
    ar1_t = A("ar1", [128, AR1_E], BF16)
    ar2_t = A("ar2", [128, AR2_E], F32)

    pbig = [nc.alloc_psum_tensor(f"pbig{i}", [128, 1536], F32) for i in range(2)]
    pb = [pbig[i // 3][:, (i % 3) * 512:(i % 3) * 512 + 512] for i in range(6)]
    pb.append(nc.alloc_psum_tensor("pb6", [128, 512], F32))
    pbt = nc.alloc_psum_tensor("pbt", [128, 1024], BF16)

    S = Sched(nc)
    ar1 = Arena(S, ar1_t, AR1_E)
    ar2 = Arena(S, ar2_t, AR2_E)

    B = Buf
    b_ident, b_gam, b_aI = B(), B(), B()
    b_onesrow, b_onescol, b_onorm = B(), B(), B()
    b_scales = B()
    b_lr, b_ssq_q, b_ssq_k, b_rq, b_rk = B(), B(), B(), B(), B()
    b_w02t, b_w1n, b_wnorm = B(), B(), B()
    b_pb = [B(excl=True) for _ in range(7)]
    b_pbt = B(excl=True)
    b_xn_h = [B(), B()]
    b_q_h = [B(), B()]
    b_kv_h = [B(), B()]
    b_hdn_h = [B(), B()]
    b_onrm_h = [B(), B()]
    b_part = B()
    b_wdbf = B()
    b_out = B()
    out_dmas = []

    S.c("gpsimd", lambda e: e.memset(ident[:, :], 0.0), writes=[b_ident])
    S.c("gpsimd", lambda e: e.affine_select(
        out=ident[:, :], in_=ident[:, :], compare_op=ALU.not_equal,
        fill=1.0, base=0, pattern=[[-1, 128]], channel_multiplier=1),
        writes=[b_ident])
    S.c("gpsimd", lambda e: e.memset(aI[:, :], 0.0), writes=[b_aI])
    for k, (a_k, _, _) in enumerate(NS_COEFS):
        S.c("gpsimd", lambda e, k=k, a_k=a_k: e.affine_select(
            out=aI[:, k * 128:(k + 1) * 128], in_=aI[:, k * 128:(k + 1) * 128],
            compare_op=ALU.not_equal, fill=float(a_k), base=0,
            pattern=[[-1, 128]], channel_multiplier=1), writes=[b_aI])
    S.c("vector", lambda e: e.memset(onesrow[:, :], 1.0), writes=[b_onesrow])
    S.c("vector", lambda e: e.memset(onescol[:, :], 1.0), writes=[b_onescol])
    S.d("sync", lambda e: e.dma_start(gate_s_sb[:, :], gate_s_r[:, :]), writes=[b_scales])
    S.d("sync", lambda e: e.dma_start(up_s_sb[:, :], up_s_r[:, :]), writes=[b_scales])
    S.d("sync", lambda e: e.dma_start(down_s_sb[:, :], down_s_r[:, :]), writes=[b_scales])
    S.d("gpsimd", lambda e: e.dma_start(onorm_bf[:, :], onorm_r[:, :]), writes=[b_onorm])
    S.d("gpsimd", lambda e: e.dma_start(w02t_sb[:, :], w02t[:, :]), writes=[b_w02t])
    S.d("gpsimd", lambda e: e.dma_start(w1n_sb[:, :], w_nat[:, 2048:4096]), writes=[b_w1n])
    S.c("tensor", lambda e: e.matmul(pb[6][:, 0:128], onesrow[:, :], onorm_bf[:, :],
                                     start=True, stop=True),
        reads=[b_onesrow, b_onorm], writes=[b_pb[6]])
    S.c("vector", lambda e: e.tensor_copy(gam[:, :], pb[6][:, 0:128]),
        reads=[b_pb[6]], writes=[b_gam])
    ar1.reset()
    ar2.reset()

    if STAGE >= 1:
        xw = [ar2.take(2048) for _ in range(2)] + \
            [ar1.take(4096).bitcast(F32) for _ in range(2)]
        b_xw = [ar2.buf(), ar2.buf(), ar1.buf(), ar1.buf()]
        st_t = ar2.take(NTC * 8)
        st3 = st_t.rearrange("p (u c) -> p u c", c=8)
        b_st = ar2.buf()
        sq_scr = ar1.take(2048)
        b_sq = ar1.buf()
        xo = [ar1.take(2048), ar1.take(2048)]
        b_xo = [ar1.buf(), ar1.buf()]
        for i in range(NTC):
            w = i % 4
            half = i // 16
            S.d("sync" if i % 2 == 0 else "gpsimd",
                lambda e, i=i, w=w: e.dma_start(
                    xw[w], x_tok[i * 128:(i + 1) * 128, :]), writes=[b_xw[w]])
            S.c("vector", lambda e, i=i, w=w: e.reduce_sum(
                st3[:, i, 0:1], xw[w], AX.X), reads=[b_xw[w]], writes=[b_st])
            S.c("scalar", lambda e, i=i, w=w: e.activation(
                sq_scr, xw[w], AF.Square, accum_out=st3[:, i, 1:2]),
                reads=[b_xw[w]], writes=[b_st, b_sq])
            S.c("vector", lambda e, i=i: e.tensor_scalar(
                st3[:, i, 2:3], st3[:, i, 0:1], float(-1.0 / D), None, ALU.mult),
                reads=[b_st], writes=[b_st])
            S.c("vector", lambda e, i=i: e.scalar_tensor_tensor(
                st3[:, i, 3:4], st3[:, i, 0:1], st3[:, i, 2:3], st3[:, i, 1:2],
                ALU.mult, ALU.add), reads=[b_st], writes=[b_st])
            S.c("scalar", lambda e, i=i: e.activation(
                st3[:, i, 4:5], st3[:, i, 3:4], AF.Sqrt,
                scale=f32(1.0 / D), bias=f32(1e-5)), reads=[b_st], writes=[b_st])
            S.c("vector", lambda e, i=i: e.reciprocal(
                st3[:, i, 5:6], st3[:, i, 4:5]), reads=[b_st], writes=[b_st])
            S.c("vector", lambda e, i=i: e.tensor_mul(
                st3[:, i, 6:7], st3[:, i, 2:3], st3[:, i, 5:6]),
                reads=[b_st], writes=[b_st])
            S.c("scalar", lambda e, i=i, w=w: e.activation(
                xo[w % 2], xw[w], AF.Identity,
                bias=st3[:, i, 6:7], scale=st3[:, i, 5:6]),
                reads=[b_xw[w], b_st], writes=[b_xo[w % 2]])
            S.d("sync", lambda e, i=i, w=w: e.dma_start(
                xn_tok3[:, i, :, :],
                xo[w % 2].rearrange("p (a c) -> p a c", a=16)),
                reads=[b_xo[w % 2]], writes=[b_xn_h[half]])
        ar1.reset()
        ar2.reset()

    wtmp = ar2.take(6144)
    wtmp3 = wtmp.rearrange("p (j h c) -> p j h c", j=3, h=H)
    b_wtmp = ar2.buf()
    junk = ar1.take(128)
    b_junk = ar1.buf()
    S.d("sync", lambda e: e.dma_start(wtmp, w_nat[:, :]), writes=[b_wtmp])
    for j in range(3):
        for h in range(H):
            S.c("vector", lambda e, j=j, h=h: e.scalar_tensor_tensor(
                junk, wtmp3[:, j, h, :], 1.0, wtmp3[:, j, h, :],
                ALU.mult, ALU.mult,
                accum_out=wnorm_sb[:, j * H + h: j * H + h + 1]),
                reads=[b_wtmp], writes=[b_wnorm, b_junk])
    S.c("scalar", lambda e: e.activation(wnorm_sb[:, :], wnorm_sb[:, :], AF.Sqrt),
        reads=[b_wnorm], writes=[b_wnorm])
    ar1.reset()
    ar2.reset()

    QOFB = 4
    NOFB = 12
    if STAGE >= 2:
        xn_t = ar1.take(16 * 2048)
        xn3 = xn_t.rearrange("p (a t) -> p a t", a=16)
        b_xn = ar1.buf()
        wq_buf = [ar1.take(16 * 512), ar1.take(16 * 512)]
        wq3 = [t.rearrange("p (a o) -> p a o", a=16) for t in wq_buf]
        b_wq = [ar1.buf(), ar1.buf()]
        ev_st = [ar1.take(512) for _ in range(3)]
        b_ev = [ar1.buf() for _ in range(3)]
        sqt = [ar1.take(512) for _ in range(2)]
        b_sqt = [ar1.buf() for _ in range(2)]
        lr_tmp = [ar2.take(NLR) for _ in range(3)]
        b_lrt = [ar2.buf() for _ in range(3)]
        r_tmp = ar2.take(256)
        b_rtmp = ar2.buf()
        wgu_buf = [ar1.take(16 * 128) for _ in range(4)]
        wgu3 = [t.rearrange("p (a o) -> p a o", a=16) for t in wgu_buf]
        b_wgu = [ar1.buf() for _ in range(4)]
        gtmp = [ar1.take(512) for _ in range(4)]
        b_gt = [ar1.buf() for _ in range(4)]
        htmp = [ar1.take(512) for _ in range(4)]
        b_ht = [ar1.buf() for _ in range(4)]
        wdc = [ar1.take(2048), ar1.take(2048)]
        wdc3 = [t.rearrange("p (a c) -> p a c", a=1) for t in wdc]
        b_wdc = [ar1.buf(), ar1.buf()]
        qp = [ar1.take(2048), ar1.take(2048)]
        qp3 = [t.rearrange("p (h c) -> p h c", h=16) for t in qp]
        b_qp = [ar1.buf(), ar1.buf()]

    def p3_half(half):
        for a in range(16):
            S.d("sync", lambda e, a=a, half=half: e.dma_start(
                xn3[:, a, :],
                xn_h[a, half * 2048:(half + 1) * 2048, :],
                transpose=True),
                reads=[b_xn_h[half]], writes=[b_xn])
        for ofb in range(NOFB + 1):
            wslot = ofb % 2
            is_lr = (ofb == NOFB)
            width = NLR if is_lr else 512
            S.d("gpsimd", lambda e, ofb=ofb, wslot=wslot, width=width: e.dma_start(
                wq3[wslot][:, :, 0:width],
                wqkv3[:, :, ofb * 512: ofb * 512 + width]),
                writes=[b_wq[wslot]])
            for tc in range(16):
                u = half * 16 + tc
                bank = tc % 3
                for a in range(16):
                    S.c("tensor", lambda e, a=a, tc=tc, wslot=wslot, width=width, bank=bank:
                        e.matmul(pb[bank][:, 0:width],
                                 xn3[:, a, tc * 128:(tc + 1) * 128],
                                 wq3[wslot][:, a, 0:width],
                                 start=(a == 0), stop=(a == 15)),
                        reads=[b_xn, b_wq[wslot]], writes=[b_pb[bank]])
                if is_lr:
                    lt = tc % 3
                    S.c("scalar", lambda e, bank=bank, lt=lt: e.activation(
                        lr_tmp[lt], pb[bank][:, 0:NLR], AF.Abs,
                        bias=f32(BASE_LR_INV)),
                        reads=[b_pb[bank]], writes=[b_lrt[lt]])
                    S.c("scalar", lambda e, lt=lt: e.activation(
                        lr_tmp[lt], lr_tmp[lt], AF.Exp, scale=f32(-1.0)),
                        reads=[b_lrt[lt]], writes=[b_lrt[lt]])
                    S.c("scalar", lambda e, lt=lt: e.activation(
                        lr_tmp[lt], lr_tmp[lt], AF.Ln, bias=f32(1.0)),
                        reads=[b_lrt[lt]], writes=[b_lrt[lt]])
                    S.c("scalar", lambda e, bank=bank, u=u: e.activation(
                        lr3[:, u, :], pb[bank][:, 0:NLR], AF.Relu,
                        bias=f32(BASE_LR_INV)),
                        reads=[b_pb[bank]], writes=[b_lr])
                    S.c("vector", lambda e, u=u, lt=lt: e.tensor_add(
                        lr3[:, u, :], lr3[:, u, :], lr_tmp[lt]),
                        reads=[b_lrt[lt]], writes=[b_lr])
                else:
                    es = tc % 3
                    S.c("scalar", lambda e, bank=bank, es=es: e.activation(
                        ev_st[es], pb[bank][:, 0:512], AF.Silu),
                        reads=[b_pb[bank]], writes=[b_ev[es]])
                    hbuf = b_q_h[half] if ofb < QOFB else b_kv_h[half]
                    dstT = (qT4 if ofb < QOFB else
                            (kT4 if ofb < 2 * QOFB else vT4))
                    g0 = half * 16 + (ofb % QOFB) * 4
                    S.d("sync", lambda e, u=u, tc=tc, g0=g0, es=es, dstT=dstT:
                        e.dma_start(
                            dstT[:, g0:g0 + 4, tc, :],
                            ev_st[es].rearrange("p (g c) -> p g c", g=4)),
                        reads=[b_ev[es]], writes=[hbuf])
                    if ofb < 8:
                        ssq3 = ssq_q3 if ofb < QOFB else ssq_k3
                        bs = b_ssq_q if ofb < QOFB else b_ssq_k
                        h0 = (ofb % 4) * 4
                        qs = tc % 2
                        ev3 = ev_st[es].rearrange("p (a b) -> p a b", b=128)
                        sq3 = sqt[qs].rearrange("p (a b) -> p a b", b=128)
                        S.c("vector", lambda e, es=es, qs=qs, ev3=ev3, sq3=sq3:
                            e.tensor_mul(sq3, ev3, ev3),
                            reads=[b_ev[es]], writes=[b_sqt[qs]])
                        S.c("vector", lambda e, qs=qs, sq3=sq3, u=u, h0=h0, ssq3=ssq3:
                            e.tensor_reduce(ssq3[:, u, h0:h0 + 4], sq3, AX.X, ALU.add),
                            reads=[b_sqt[qs]], writes=[bs])
        for (ssq3v, bs, r3v, br) in ((ssq_q3, b_ssq_q, rq3, b_rq),
                                     (ssq_k3, b_ssq_k, rk3, b_rk)):
            sl = slice(half * 16, (half + 1) * 16)
            S.c("scalar", lambda e, ssq3v=ssq3v, sl=sl, bs=bs: e.activation(
                r_tmp, ssq3v[:, sl, :].rearrange("p a b -> p (a b)"), AF.Ln,
                bias=f32(1e-30)), reads=[bs], writes=[b_rtmp])
            S.c("scalar", lambda e: e.activation(
                r_tmp, r_tmp, AF.Exp, scale=f32(0.5)),
                reads=[b_rtmp], writes=[b_rtmp])
            S.c("vector", lambda e: e.tensor_scalar(
                r_tmp, r_tmp, float(1e-5), None, ALU.add),
                reads=[b_rtmp], writes=[b_rtmp])
            S.c("vector", lambda e, r3v=r3v, sl=sl, br=br: e.reciprocal(
                r3v[:, sl, :].rearrange("p a b -> p (a b)"), r_tmp),
                reads=[b_rtmp], writes=[br])
        for tc in range(16):
            u = half * 16 + tc
            w = tc % 2
            g0 = half * 16
            S.d("sync", lambda e, tc=tc, w=w, g0=g0: e.dma_start(
                qp3[w], qT4[:, g0:g0 + 16, tc, :]),
                reads=[b_q_h[half]], writes=[b_qp[w]])
            S.c("vector", lambda e, u=u, w=w: e.tensor_mul(
                qp3[w], qp3[w],
                rq3[:, u, :][:, :, None].broadcast_to([128, 16, 128])),
                reads=[b_rq, b_qp[w]], writes=[b_qp[w]])
            S.d("sync", lambda e, tc=tc, w=w, g0=g0: e.dma_start(
                qT4[:, g0:g0 + 16, tc, :], qp3[w]),
                reads=[b_qp[w]], writes=[b_q_h[half]])

    def p3b_half(half):
        bank_map = [[0, 1, 2, 3], [4, 5, 6, 3]]
        for of in range(NI):
            sg = of % 2
            S.d("gpsimd", lambda e, of=of, sg=sg: e.dma_start(
                wgu3[2 * sg], wg3[:, :, of * 128:(of + 1) * 128]),
                writes=[b_wgu[2 * sg]])
            S.d("gpsimd", lambda e, of=of, sg=sg: e.dma_start(
                wgu3[2 * sg + 1], wu3[:, :, of * 128:(of + 1) * 128]),
                writes=[b_wgu[2 * sg + 1]])
            for tq in range(2):
                bm = bank_map[tq]
                for gi in range(2):
                    wb = b_wgu[2 * sg + gi]
                    wv = wgu3[2 * sg + gi]
                    for a in range(16):
                        for t in range(2):
                            bank = bm[gi * 2 + t]
                            S.c("tensor", lambda e, a=a, t=t, tq=tq, wv=wv, bank=bank:
                                e.matmul(pb[bank][:, 0:512], wv[:, a, :],
                                         xn3[:, a, tq * 1024 + t * 512:
                                             tq * 1024 + (t + 1) * 512],
                                         start=(a == 0), stop=(a == 15)),
                                reads=[b_xn, wb], writes=[b_pb[bank]])
                for t in range(2):
                    es = 2 * tq + t
                    gb, ub = bm[t], bm[2 + t]
                    S.c("scalar", lambda e, of=of, es=es, gb=gb: e.activation(
                        gtmp[es], pb[gb][:, 0:512], AF.Silu,
                        scale=gate_s_sb[:, of:of + 1]),
                        reads=[b_pb[gb], b_scales], writes=[b_gt[es]])
                    S.c("vector", lambda e, of=of, es=es, ub=ub: e.scalar_tensor_tensor(
                        htmp[es], pb[ub][:, 0:512], up_s_sb[:, of:of + 1],
                        gtmp[es], ALU.mult, ALU.mult),
                        reads=[b_pb[ub], b_gt[es], b_scales], writes=[b_ht[es]])
                    off = half * 2048 + tq * 1024 + t * 512
                    S.d("sync", lambda e, of=of, off=off, es=es: e.dma_start(
                        hdn3[:, of, off:off + 512], htmp[es]),
                        reads=[b_ht[es]], writes=[b_hdn_h[half]])
        if half == 0:
            for cc in range(NI):
                w = cc % 2
                S.d("gpsimd", lambda e, cc=cc, w=w: e.dma_start(
                    wdc3[w], wd3[:, cc:cc + 1, :]), writes=[b_wdc[w]])
                S.d("sync", lambda e, cc=cc, w=w: e.dma_start(
                    wdbf3[:, cc:cc + 1, :], wdc3[w]),
                    reads=[b_wdc[w]], writes=[b_wdbf])

    if STAGE >= 2:
        p3_half(0)
        if STAGE >= 3:
            p3b_half(0)
        p3_half(1)
        if STAGE >= 3:
            p3b_half(1)
        ar1.reset()
        ar2.reset()

    if STAGE >= 4:
        NMAT = 3 * P4_HM
        DWS_t = ar1.take(96 * 128)
        DWS3 = DWS_t.rearrange("p (n c) -> p n c", n=96)
        b_dw = [ar1.buf() for _ in range(96)]
        xts_off = ar1.cur
        XTS_t = ar1.take(96 * 128)
        XTS3 = XTS_t.rearrange("p (n c) -> p n c", n=96)
        b_xt96 = [ar1.buf() for _ in range(96)]
        csb96 = ar1.take(128)
        b_csb = ar1.buf()
        cs96 = ar2.take(96)
        b_cs = ar2.buf()
        scl96 = ar2.take(96)
        b_scl = ar2.buf()
        ar1_mark = ar1.cur
        ar2_mark = ar2.cur

        par = []
        for p in range(2):
            sl = {}
            for name in ("KT", "VT", "SG", "DS", "HID", "DGA", "DB2"):
                sl[name] = ar1.take(2048)
            sl["GH"] = ar1.take(16 * 384)
            sm = {"lrk": ar2.take(32)}
            bufs = {k: ar1.buf() for k in list(sl) + ["lrk"]}
            bufs["DH"] = bufs["GH"]
            par.append((sl, sm, bufs))
        kfr = [ar1.take(2048) for _ in range(3)]
        vfr = [ar1.take(2048) for _ in range(3)]
        b_kfr = [ar1.buf() for _ in range(3)]
        b_vfr = [ar1.buf() for _ in range(3)]

        def p4a_streams(hm):
            half = hm // H
            h = hm % H
            hh = half * 16 + h
            r = hm % 3
            S.d("sync", lambda e, hh=hh, r=r: e.dma_start(
                kfr[r], kT_h[hh], transpose=True),
                reads=[b_kv_h[half]], writes=[b_kfr[r]])
            S.d("sync", lambda e, hh=hh, r=r: e.dma_start(
                vfr[r], vT_h[hh], transpose=True),
                reads=[b_kv_h[half]], writes=[b_vfr[r]])

        def p4a(hm):
            half = hm // H
            h = hm % H
            p = hm % 2
            r = hm % 3
            sl, sm, bb = par[p]
            bb = dict(bb)
            bb["KF"] = b_kfr[r]
            bb["VF"] = b_vfr[r]
            bank = [b_pb[3 * p + i] for i in range(3)]
            pbk = [pb[3 * p + i] for i in range(3)]
            u0 = half * 16
            KF, VF, KT, VT = kfr[r], vfr[r], sl["KT"], sl["VT"]
            GH3 = sl["GH"].rearrange("p (a c) -> p a c", a=16)
            gba = GH3[:, :, 0:128]
            hbm = GH3[:, :, 128:256]
            gbahbm = GH3[:, :, 0:256]
            DH3 = GH3[:, :, 256:384]
            KF3 = KF.rearrange("p (a c) -> p a c", a=16)
            VF3 = VF.rearrange("p (a c) -> p a c", a=16)
            KT3 = KT.rearrange("p (a c) -> p a c", a=16)
            VT3 = VT.rearrange("p (a c) -> p a c", a=16)
            SG3 = sl["SG"].rearrange("p (a c) -> p a c", a=16)
            DS3 = sl["DS"].rearrange("p (a c) -> p a c", a=16)
            HID3 = sl["HID"].rearrange("p (a c) -> p a c", a=16)
            DGA3 = sl["DGA"].rearrange("p (a c) -> p a c", a=16)
            DB23 = sl["DB2"].rearrange("p (a c) -> p a c", a=16)

            hh = half * 16 + h
            S.d("sync", lambda e, hh=hh: e.dma_start(
                KT3, kT4[:, hh, :, :]),
                reads=[b_kv_h[half]], writes=[bb["KT"]])
            S.d("sync", lambda e, hh=hh: e.dma_start(
                VT3, vT4[:, hh, :, :]),
                reads=[b_kv_h[half]], writes=[bb["VT"]])

            pgrp = pbig[p]
            for w0sc in range(0, 16, 3):
                ns = min(3, 16 - w0sc)
                for i in range(ns):
                    sc = w0sc + i
                    pk = pbk[i]
                    S.c("tensor", lambda e, sc=sc, pk=pk: e.matmul(
                        pk[:, 0:256], KF3[:, sc, :], w02t_sb3[:, h, :],
                        start=True, stop=True),
                        reads=[bb["KF"], b_w02t], writes=[bank[i]])
                    S.c("tensor", lambda e, sc=sc, pk=pk: e.matmul(
                        pk[:, 256:384], VF3[:, sc, :], w1n_sb3[:, h, :],
                        start=True, stop=True),
                        reads=[bb["VF"], b_w1n], writes=[bank[i]])
                span = pgrp[:, 0:3 * 512].rearrange(
                    "p (b c) -> p b c", b=3)[:, 0:ns, 0:384]
                S.c("scalar", lambda e, w0sc=w0sc, ns=ns, span=span: e.activation(
                    GH3[:, w0sc:w0sc + ns, :], span, AF.Identity),
                    reads=[], writes=[bank[0], bank[1], bank[2], bb["GH"]])

            S.c("vector", lambda e: e.tensor_mul(
                gbahbm, gbahbm,
                rk3[:, u0:u0 + 16, h:h + 1].broadcast_to([128, 16, 256])),
                reads=[b_rk], writes=[bb["GH"]])
            S.c("scalar", lambda e: e.activation(DS3, gba, AF.Sigmoid),
                reads=[bb["GH"]], writes=[bb["DS"]])
            S.c("vector", lambda e: e.tensor_mul(SG3, gba, DS3),
                reads=[bb["GH"], bb["DS"]], writes=[bb["SG"]])
            S.c("vector", lambda e: e.tensor_mul(HID3, SG3, hbm),
                reads=[bb["SG"], bb["GH"]], writes=[bb["HID"]])
            S.c("gpsimd", lambda e: e.tensor_mul(DGA3, DH3, hbm),
                reads=[bb["DH"], bb["GH"]], writes=[bb["DGA"]])
            S.c("gpsimd", lambda e: e.tensor_mul(DB23, DH3, SG3),
                reads=[bb["DH"], bb["SG"]], writes=[bb["DB2"]])
            S.c("vector", lambda e: e.tensor_tensor(gba, gba, SG3, ALU.subtract),
                reads=[bb["SG"]], writes=[bb["GH"]])
            S.c("vector", lambda e: e.scalar_tensor_tensor(
                DS3, gba, 1.0, DS3, ALU.add, ALU.mult),
                reads=[bb["GH"]], writes=[bb["DS"]])
            S.c("vector", lambda e: e.tensor_mul(sl["DGA"], sl["DGA"], sl["DS"]),
                reads=[bb["DS"]], writes=[bb["DGA"]])
            lrk3 = sm["lrk"].rearrange("p (a c) -> p a c", a=2)
            S.c("vector", lambda e: e.tensor_mul(
                lrk3[:, 0, :], lr3[:, u0:u0 + 16, 3 * h], rk3[:, u0:u0 + 16, h]),
                reads=[b_lr, b_rk], writes=[bb["lrk"]])
            S.c("vector", lambda e: e.tensor_mul(
                lrk3[:, 1, :], lr3[:, u0:u0 + 16, 3 * h + 2], rk3[:, u0:u0 + 16, h]),
                reads=[b_lr, b_rk], writes=[bb["lrk"]])
            S.c("vector", lambda e: e.tensor_mul(
                KF3, KT3, lrk3[:, 1, :][:, :, None].broadcast_to([128, 16, 128])),
                reads=[bb["KT"], bb["lrk"]], writes=[bb["KF"]])
            S.c("gpsimd", lambda e: e.tensor_mul(
                KT3, KT3, lrk3[:, 0, :][:, :, None].broadcast_to([128, 16, 128])),
                reads=[bb["lrk"]], writes=[bb["KT"]])
            S.c("vector", lambda e: e.tensor_mul(
                HID3, HID3,
                lr3[:, u0:u0 + 16, 3 * h + 1][:, :, None].broadcast_to([128, 16, 128])),
                reads=[b_lr], writes=[bb["HID"]])

        def p4a_dw(hm):
            p = hm % 2
            r = hm % 3
            sl, sm, bb = par[p]
            bb = dict(bb)
            bb["KF"] = b_kfr[r]
            bank = [b_pb[3 * p + i] for i in range(3)]
            pbk = [pb[3 * p + i] for i in range(3)]
            KF3 = kfr[r].rearrange("p (a c) -> p a c", a=16)
            KT3 = sl["KT"].rearrange("p (a c) -> p a c", a=16)
            VT3 = sl["VT"].rearrange("p (a c) -> p a c", a=16)
            HID3 = sl["HID"].rearrange("p (a c) -> p a c", a=16)
            DGA3 = sl["DGA"].rearrange("p (a c) -> p a c", a=16)
            DB23 = sl["DB2"].rearrange("p (a c) -> p a c", a=16)
            ops = ((DGA3, KT3), (VT3, HID3), (DB23, KF3))
            nmA = ("DGA", "VT", "DB2")
            nmB = ("KT", "HID", "KF")
            for j in range(3):
                A3, B3 = ops[j]
                for sc in range(16):
                    S.c("tensor", lambda e, sc=sc, j=j, A3=A3, B3=B3, pbk=pbk:
                        e.matmul(pbk[j][:, 0:128], A3[:, sc, :], B3[:, sc, :],
                                 start=(sc == 0), stop=(sc == 15)),
                        reads=[bb[nmA[j]], bb[nmB[j]]], writes=[bank[j]])
                for sc in range(16):
                    S.c("tensor", lambda e, sc=sc, j=j, A3=A3, B3=B3, pbk=pbk:
                        e.matmul(pbk[j][:, 128:256], B3[:, sc, :], A3[:, sc, :],
                                 start=(sc == 0), stop=(sc == 15)),
                        reads=[bb[nmA[j]], bb[nmB[j]]], writes=[bank[j]])
                idx = hm * 3 + j
                S.c("vector", lambda e, j=j, idx=idx, pbk=pbk: e.tensor_copy(
                    DWS3[:, idx, :], pbk[j][:, 0:128]),
                    reads=[], writes=[bank[j], b_dw[idx]])
                S.c("scalar", lambda e, j=j, idx=idx, pbk=pbk: e.activation(
                    XTS3[:, idx, :], pbk[j][:, 128:256], AF.Identity),
                    reads=[], writes=[bank[j], b_xt96[idx]])
                S.c("vector", lambda e, j=j, idx=idx: e.scalar_tensor_tensor(
                    sl["SG"][:, 0:128], DWS3[:, idx, :], 1.0, DWS3[:, idx, :],
                    ALU.mult, ALU.mult, accum_out=cs96[:, idx:idx + 1]),
                    reads=[b_dw[idx]], writes=[b_cs, bb["SG"]])

        p4a_streams(0)
        p4a_streams(1)
        for hm in range(P4_HM):
            p4a(hm)
            if hm > 0:
                p4a_dw(hm - 1)
            if hm + 2 < P4_HM:
                p4a_streams(hm + 2)
        p4a_dw(P4_HM - 1)

        fence_b = tuple(S.cut())
        ar1.cur = ar1_mark
        hb_t = ar1.take(NI * 512)
        hb3 = hb_t.rearrange("p (a t) -> p a t", a=NI)
        b_hb = Buf(fence_b)
        wdb = [ar1.take(NI * 128), ar1.take(NI * 128)]
        wdb3 = [t.rearrange("p (a c) -> p a c", a=NI) for t in wdb]
        b_wdb = [Buf(fence_b), Buf(fence_b)]
        ffs = [ar2.take(512), ar2.take(512)]
        b_ffs = [ar2.buf(), ar2.buf()]

        def p6_units():
            for tb in range(8):
                half = tb // 4
                for seg in range(4):
                    S.d("sync", lambda e, tb=tb, seg=seg: e.dma_start(
                        hb3[:, seg * 11:(seg + 1) * 11, :],
                        hdn3[:, seg * 11:(seg + 1) * 11,
                             tb * 512:(tb + 1) * 512]),
                        reads=[b_hdn_h[half]], writes=[b_hb])
                for oc in range(16):
                    ws = oc % 2
                    S.d("gpsimd", lambda e, oc=oc, ws=ws: e.dma_start(
                        wdb3[ws], wdbf3[:, :, oc * 128:(oc + 1) * 128]),
                        reads=[b_wdbf], writes=[b_wdb[ws]])
                    for a in range(NI):
                        S.c("tensor", lambda e, a=a, ws=ws: e.matmul(
                            pb[6][:, 0:512], wdb3[ws][:, a, :],
                            hb3[:, a, :],
                            start=(a == 0), stop=(a == NI - 1)),
                            reads=[b_wdb[ws], b_hb], writes=[b_pb[6]])
                    fs = oc % 2
                    S.c("vector", lambda e, oc=oc, fs=fs:
                        e.tensor_scalar(
                            ffs[fs], pb[6][:, 0:512],
                            down_s_sb[:, oc:oc + 1], None, ALU.mult),
                        reads=[b_scales], writes=[b_pb[6], b_ffs[fs]])
                    tg = tb * 512
                    S.d("sync", lambda e, oc=oc, tg=tg, fs=fs: e.dma_start(
                        part_h[oc * 128:(oc + 1) * 128, tg:tg + 512], ffs[fs]),
                        reads=[b_ffs[fs]], writes=[b_part])
                    yield None

        p6_gen = p6_units() if STAGE >= 5 else iter(())

        def p6_feed(n):
            for _ in range(n):
                try:
                    next(p6_gen)
                except StopIteration:
                    return

        nsb_t = [ar1.take(3 * 128) for _ in range(2)]
        nsB_t = [ar1.take(3 * 128) for _ in range(2)]
        nsb3 = [t.rearrange("p (b c) -> p b c", b=3) for t in nsb_t]
        nsB3 = [t.rearrange("p (b c) -> p b c", b=3) for t in nsB_t]
        b_nsb = [Buf(fence_b), Buf(fence_b)]
        b_nsB = [Buf(fence_b), Buf(fence_b)]
        S.c("vector", lambda e: e.tensor_copy(csb96[:, 0:96], cs96),
            reads=[b_cs], writes=[b_csb])
        S.c("tensor", lambda e: e.matmul(
            pb[6][0:1, 0:96], onescol[:, :], csb96[:, 0:96],
            start=True, stop=True),
            reads=[b_csb, b_onescol], writes=[b_pb[6]])
        S.c("scalar", lambda e: e.activation(
            scl96[0:1, :], pb[6][0:1, 0:96], AF.Sqrt),
            reads=[], writes=[b_pb[6], b_scl])
        S.c("vector", lambda e: e.tensor_scalar(
            scl96[0:1, :], scl96[0:1, :], float(1e-7), None, ALU.add),
            reads=[], writes=[b_scl])
        S.c("vector", lambda e: e.reciprocal(scl96[0:1, :], scl96[0:1, :]),
            reads=[], writes=[b_scl])
        S.c("vector", lambda e: e.tensor_copy(csb96[0:1, 0:96], scl96[0:1, :]),
            reads=[b_scl], writes=[b_csb])
        S.c("tensor", lambda e: e.matmul(
            pb[6][:, 128:224], onesrow[:, :], csb96[0:1, 0:96],
            start=True, stop=True),
            reads=[b_csb, b_onesrow], writes=[b_pb[6]])
        S.c("vector", lambda e: e.tensor_copy(scl96[:, :], pb[6][:, 128:224]),
            reads=[], writes=[b_pb[6], b_scl])
        S.c("vector", lambda e: e.tensor_mul(
            DWS3[:, 0:NMAT, :], DWS3[:, 0:NMAT, :],
            scl96[:, 0:NMAT][:, :, None].broadcast_to([128, NMAT, 128])),
            reads=[b_scl] + b_dw[:NMAT], writes=b_dw[:NMAT])
        S.c("vector", lambda e: e.tensor_mul(
            XTS3[:, 0:NMAT, :], XTS3[:, 0:NMAT, :],
            scl96[:, 0:NMAT][:, :, None].broadcast_to([128, NMAT, 128])),
            reads=[b_scl] + b_xt96[:NMAT], writes=b_xt96[:NMAT])
        for k, (a_k, b_k, c_k) in enumerate(NS_COEFS):
            cb2 = float(c_k / (b_k * b_k))
            for g0 in range(0, NMAT, 6):
                p6_feed(1)
                grp = list(range(g0, min(g0 + 6, NMAT)))
                halves = [[i for i in grp if i % 6 < 3],
                          [i for i in grp if i % 6 >= 3]]
                def bankof(idx):
                    return b_pb[idx % 6], pb[idx % 6]
                for idx in grp:
                    bk, pk = bankof(idx)
                    xtap = XTS3[:, idx, :]
                    S.c("tensor", lambda e, pk=pk, xtap=xtap: e.matmul(
                        pk[:, 0:128], xtap, xtap, start=True, stop=True),
                        reads=[b_xt96[idx]], writes=[bk])
                for hf in range(2):
                    ids = halves[hf]
                    if not ids:
                        continue
                    n = len(ids)
                    span = pbig[hf][:, 0:1536].rearrange(
                        "p (b c) -> p b c", b=3)[:, 0:n, 0:128]
                    S.c("vector", lambda e, hf=hf, n=n, span=span, b_k=b_k:
                        e.tensor_scalar(nsb3[hf][:, 0:n, :], span, float(b_k),
                                        None, ALU.mult),
                        reads=[], writes=[b_pb[3 * hf], b_pb[3 * hf + 1],
                                          b_pb[3 * hf + 2], b_nsb[hf]])
                for idx in grp:
                    bk, pk = bankof(idx)
                    hf = 0 if idx % 6 < 3 else 1
                    bq = idx % 3
                    S.c("tensor", lambda e, pk=pk, hf=hf, bq=bq: e.matmul(
                        pk[:, 128:256], nsb3[hf][:, bq, :], nsb3[hf][:, bq, :],
                        start=True, stop=True),
                        reads=[b_nsb[hf]], writes=[bk])
                for hf in range(2):
                    ids = halves[hf]
                    if not ids:
                        continue
                    n = len(ids)
                    span = pbig[hf][:, 0:1536].rearrange(
                        "p (b c) -> p b c", b=3)[:, 0:n, 128:256]
                    S.c("vector", lambda e, hf=hf, n=n, span=span, cb2=cb2:
                        e.scalar_tensor_tensor(
                            nsB3[hf][:, 0:n, :], span, cb2, nsb3[hf][:, 0:n, :],
                            ALU.mult, ALU.add),
                        reads=[b_nsb[hf]], writes=[b_pb[3 * hf], b_pb[3 * hf + 1],
                                                   b_pb[3 * hf + 2], b_nsB[hf]])
                for idx in grp:
                    bk, pk = bankof(idx)
                    hf = 0 if idx % 6 < 3 else 1
                    bq = idx % 3
                    xap = DWS3[:, idx, :]
                    S.c("tensor", lambda e, pk=pk, hf=hf, bq=bq, xap=xap: e.matmul(
                        pk[:, 256:384], nsB3[hf][:, bq, :], xap,
                        start=True, stop=False),
                        reads=[b_nsB[hf], b_dw[idx]], writes=[bk])
                    S.c("tensor", lambda e, pk=pk, k=k, xap=xap: e.matmul(
                        pk[:, 256:384], aI[:, k * 128:(k + 1) * 128], xap,
                        start=False, stop=True),
                        reads=[b_aI, b_dw[idx]], writes=[bk])
                for idx in grp:
                    bk, pk = bankof(idx)
                    hf = 0 if idx % 6 < 3 else 1
                    bq = idx % 3
                    xap = DWS3[:, idx, :]
                    S.c("tensor", lambda e, pk=pk, hf=hf, bq=bq, xap=xap: e.matmul(
                        pk[:, 384:512], xap, nsB3[hf][:, bq, :],
                        start=True, stop=False),
                        reads=[b_nsB[hf], b_dw[idx]], writes=[bk])
                    S.c("tensor", lambda e, pk=pk, k=k, xap=xap: e.matmul(
                        pk[:, 384:512], xap, aI[:, k * 128:(k + 1) * 128],
                        start=False, stop=True),
                        reads=[b_aI, b_dw[idx]], writes=[bk])
                for hf in range(2):
                    ids = halves[hf]
                    if not ids:
                        continue
                    n = len(ids)
                    lo = ids[0]
                    span = pbig[hf][:, 0:1536].rearrange(
                        "p (b c) -> p b c", b=3)[:, 0:n, 256:384]
                    S.c("scalar", lambda e, span=span, lo=lo, n=n: e.activation(
                        DWS3[:, lo:lo + n, :], span, AF.Identity),
                        reads=[], writes=[b_pb[3 * hf], b_pb[3 * hf + 1],
                                          b_pb[3 * hf + 2]] + b_dw[lo:lo + n])
                    span2 = pbig[hf][:, 0:1536].rearrange(
                        "p (b c) -> p b c", b=3)[:, 0:n, 384:512]
                    S.c("vector", lambda e, span2=span2, lo=lo, n=n: e.tensor_copy(
                        XTS3[:, lo:lo + n, :], span2),
                        reads=[], writes=[b_pb[3 * hf], b_pb[3 * hf + 1],
                                          b_pb[3 * hf + 2]] + b_xt96[lo:lo + n])

        fence_c = tuple(S.cut())
        parc = []
        for p in range(2):
            ar1.cur = xts_off if p == 0 else 60160
            sl = {}
            for name in ("QF", "GHS", "SIL", "OS", "OSQ"):
                sl[name] = ar1.take(2048)
            sl["WT"] = ar1.take(3 * 128)
            sm = {"WN": ar2.take(3 * 128), "WPD": ar2.take(3 * 128),
                  "rc": ar2.take(16), "oss": ar2.take(64)}
            bufs = {}
            for k in list(sl) + list(sm):
                bufs[k] = Buf(fence_c)
            parc.append((sl, sm, bufs))

        def p4c_pre(hm):
            half = hm // H
            h = hm % H
            p = hm % 2
            sl, sm, bb = parc[p]
            QF = sl["QF"]
            WT3 = sl["WT"].rearrange("p (a c) -> p a c", a=3)
            WN3 = sm["WN"].rearrange("p (a c) -> p a c", a=3)
            WPD3 = sm["WPD"].rearrange("p (a c) -> p a c", a=3)
            dsl = DWS3[:, hm * 3:(hm + 1) * 3, :]
            dbufs = [b_dw[hm * 3 + j] for j in range(3)]
            hh = half * 16 + h
            S.d("sync", lambda e, hh=hh: e.dma_start(
                QF, qT_h[hh], transpose=True),
                reads=[b_q_h[half]], writes=[bb["QF"]])
            S.d("sync", lambda e: e.dma_start(WN3, wnat3[:, :, h, :]),
                writes=[bb["WN"]])
            S.c("vector", lambda e: e.tensor_tensor(WPD3, WN3, dsl, ALU.add),
                reads=[bb["WN"]] + dbufs, writes=[bb["WPD"]])
            for j in range(3):
                S.c("vector", lambda e, j=j: e.scalar_tensor_tensor(
                    sl["OSQ"][:, 0:128], WPD3[:, j, :], 1.0, WPD3[:, j, :],
                    ALU.mult, ALU.mult, accum_out=sm["rc"][:, j:j + 1]),
                    reads=[bb["WPD"]], writes=[bb["rc"], bb["OSQ"]])
            S.c("scalar", lambda e: e.activation(
                sm["rc"][:, 4:7], sm["rc"][:, 0:3], AF.Sqrt),
                reads=[bb["rc"]], writes=[bb["rc"]])
            S.c("vector", lambda e: e.tensor_scalar(
                sm["rc"][:, 4:7], sm["rc"][:, 4:7], float(1e-5), None, ALU.add),
                reads=[], writes=[bb["rc"]])
            S.c("vector", lambda e: e.reciprocal(
                sm["rc"][:, 8:11], sm["rc"][:, 4:7]),
                reads=[], writes=[bb["rc"]])
            S.c("vector", lambda e, h=h: e.tensor_mul(
                sm["rc"][:, 8:11], sm["rc"][:, 8:11], wnorm_sb[:, h:48:16]),
                reads=[b_wnorm], writes=[bb["rc"]])
            S.c("vector", lambda e: e.tensor_mul(
                WT3, WPD3,
                sm["rc"][:, 8:11][:, :, None].broadcast_to([128, 3, 128])),
                reads=[bb["WPD"], bb["rc"]], writes=[bb["WT"]])
            for j in range(3):
                S.c("tensor", lambda e, j=j: e.transpose(
                    pbt[:, 0:128], WT3[:, j, :], ident[:, :]),
                    reads=[bb["WT"], b_ident], writes=[b_pbt])
                S.c("vector", lambda e, j=j: e.tensor_copy(
                    WT3[:, j, :], pbt[:, 0:128]),
                    reads=[], writes=[b_pbt, bb["WT"]])

        def p4c_apply(hm):
            half = hm // H
            h = hm % H
            p = hm % 2
            sl, sm, bb = parc[p]
            bank = [b_pb[3 * p + i] for i in range(2)]
            pbk = [pb[3 * p + i] for i in range(2)]
            u0 = half * 16
            QF = sl["QF"]
            GHS = sl["GHS"]
            SIL = sl["SIL"]
            OS = sl["OS"]
            OS3 = sl["OS"].rearrange("p (a c) -> p a c", a=16)
            OSQ3 = sl["OSQ"].rearrange("p (a c) -> p a c", a=16)
            WT3 = sl["WT"].rearrange("p (a c) -> p a c", a=3)
            for t in range(4):
                S.c("tensor", lambda e, t=t: e.matmul(
                    pbk[0][:, 0:512], WT3[:, 0, :], QF[:, t * 512:(t + 1) * 512],
                    start=True, stop=True),
                    reads=[bb["WT"], bb["QF"]], writes=[bank[0]])
                S.c("tensor", lambda e, t=t: e.matmul(
                    pbk[1][:, 0:512], WT3[:, 2, :], QF[:, t * 512:(t + 1) * 512],
                    start=True, stop=True),
                    reads=[bb["WT"], bb["QF"]], writes=[bank[1]])
                S.c("scalar", lambda e, t=t: e.activation(
                    SIL[:, t * 512:(t + 1) * 512], pbk[0][:, 0:512], AF.Silu),
                    reads=[], writes=[bank[0], bb["SIL"]])
                S.c("vector", lambda e, t=t: e.scalar_tensor_tensor(
                    GHS[:, t * 512:(t + 1) * 512], pbk[1][:, 0:512], 1.0,
                    SIL[:, t * 512:(t + 1) * 512], ALU.bypass, ALU.mult),
                    reads=[bb["SIL"]], writes=[bank[1], bb["GHS"]])
            for scg in range(4):
                bk = bank[scg % 2]
                pk = pbk[scg % 2]
                for si in range(4):
                    sc = scg * 4 + si
                    S.c("tensor", lambda e, sc=sc, si=si, pk=pk: e.matmul(
                        pk[:, si * 128:(si + 1) * 128],
                        GHS[:, sc * 128:(sc + 1) * 128], WT3[:, 1, :],
                        start=True, stop=True),
                        reads=[bb["GHS"], bb["WT"]], writes=[bk])
                S.c("vector", lambda e, scg=scg, pk=pk: e.tensor_copy(
                    OS[:, scg * 512:(scg + 1) * 512], pk[:, 0:512]),
                    reads=[], writes=[bk, bb["OS"]])
            oss3 = sm["oss"].rearrange("p (a c) -> p a c", a=4)
            S.c("vector", lambda e: e.tensor_mul(OSQ3, OS3, OS3),
                reads=[bb["OS"]], writes=[bb["OSQ"]])
            S.c("vector", lambda e: e.tensor_reduce(
                oss3[:, 0, :], OSQ3, AX.X, ALU.add),
                reads=[bb["OSQ"]], writes=[bb["oss"]])
            S.c("scalar", lambda e: e.activation(
                oss3[:, 1, :], oss3[:, 0, :], AF.Sqrt,
                scale=f32(1.0 / 128.0), bias=f32(1e-5)),
                reads=[], writes=[bb["oss"]])
            S.c("vector", lambda e: e.reciprocal(
                oss3[:, 2, :], oss3[:, 1, :]),
                reads=[], writes=[bb["oss"]])
            S.c("vector", lambda e: e.tensor_mul(
                OS3, OS3, oss3[:, 2, :][:, :, None].broadcast_to([128, 16, 128])),
                reads=[bb["oss"]], writes=[bb["OS"]])
            S.c("vector", lambda e: e.tensor_mul(
                OS3, OS3, gam[:, None, :].broadcast_to([128, 16, 128])),
                reads=[b_gam], writes=[bb["OS"]])
            S.d("sync", lambda e, half=half, h=h: e.dma_start(
                onrm4[:, half * 16 + h, :, :], OS3),
                reads=[bb["OS"]], writes=[b_onrm_h[half]])

        for hm in range(P4_HM):
            p4c_pre(hm)
            if hm > 0:
                p4c_apply(hm - 1)
            p6_feed(2)
        p4c_apply(P4_HM - 1)
        p6_feed(128)
        ar1.reset()
        ar2.reset()

    if STAGE >= 5:
        of_t = ar1.take(16 * 2048)
        of3 = of_t.rearrange("p (h t) -> p h t", h=16)
        b_of = ar1.buf()
        wo_buf = [ar1.take(16 * 128), ar1.take(16 * 128)]
        wo_b3 = [t.rearrange("p (a c) -> p a c", a=16) for t in wo_buf]
        b_wo = [ar1.buf(), ar1.buf()]
        xt_buf = [ar2.take(512) for _ in range(4)]
        b_xt = [ar2.buf() for _ in range(4)]
        ft_buf = [ar2.take(512) for _ in range(4)]
        b_ft = [ar2.buf() for _ in range(4)]
        po_buf = [ar2.take(512) for _ in range(4)]
        b_po = [ar2.buf() for _ in range(4)]
        for half in range(2):
            t0 = half * 2048
            for hh in range(16):
                S.d("sync", lambda e, hh=hh, half=half: e.dma_start(
                    of3[:, hh, :], onrm_h[half * 16 + hh], transpose=True),
                    reads=[b_onrm_h[half]], writes=[b_of])
            for oc in range(16):
                ws = oc % 2
                S.d("gpsimd", lambda e, oc=oc, ws=ws: e.dma_start(
                    wo_b3[ws], wo3[:, :, oc * 128:(oc + 1) * 128]),
                    writes=[b_wo[ws]])
                for t in range(4):
                    for hh in range(16):
                        S.c("tensor", lambda e, hh=hh, t=t, ws=ws: e.matmul(
                            pb[t][:, 0:512], wo_b3[ws][:, hh, :],
                            of3[:, hh, t * 512:(t + 1) * 512],
                            start=(hh == 0), stop=(hh == 15)),
                            reads=[b_wo[ws], b_of], writes=[b_pb[t]])
                    S.d("sync", lambda e, oc=oc, t=t, t0=t0: e.dma_start(
                        xt_buf[t], x_feat[oc * 128:(oc + 1) * 128,
                                          t0 + t * 512: t0 + (t + 1) * 512]),
                        writes=[b_xt[t]])
                    S.d("sync", lambda e, oc=oc, t=t, t0=t0: e.dma_start(
                        ft_buf[t], part_h[oc * 128:(oc + 1) * 128,
                                          t0 + t * 512: t0 + (t + 1) * 512]),
                        reads=[b_part], writes=[b_ft[t]])
                    S.c("vector", lambda e, t=t: e.scalar_tensor_tensor(
                        po_buf[t], pb[t][:, 0:512], 1.0, xt_buf[t],
                        ALU.bypass, ALU.add),
                        reads=[b_xt[t]], writes=[b_pb[t], b_po[t]])
                    S.c("vector", lambda e, t=t: e.tensor_add(
                        po_buf[t], po_buf[t], ft_buf[t]),
                        reads=[b_ft[t]], writes=[b_po[t]])
                    oid = S.d("sync", lambda e, oc=oc, t=t, t0=t0: e.dma_start(
                        out_feat[oc * 128:(oc + 1) * 128,
                                 t0 + t * 512: t0 + (t + 1) * 512], po_buf[t]),
                        reads=[b_po[t]], writes=[b_out])
                    out_dmas.append(oid)
        ar1.reset()
        ar2.reset()

    S.emit(sorted(set(out_dmas) | set(S.cut())))
    return nc


_NC_CACHE = None


def _get_nc():
    global _NC_CACHE
    if _NC_CACHE is None:
        _NC_CACHE = build_nc()
    return _NC_CACHE


def _prep_shared(ins):
    w0 = np.asarray(ins["L0_attn_w0"])
    w1 = np.asarray(ins["L0_attn_w1"])
    w2 = np.asarray(ins["L0_attn_w2"])
    return {
        "wqkv_t": np.ascontiguousarray(
            np.concatenate([np.asarray(ins["L0_attn_to_qkv_w"]),
                            np.asarray(ins["L0_attn_lr_proj_w"])], 0).T),
        "wg_t": np.ascontiguousarray(np.asarray(ins["L0_ffn_gate_w"]).T),
        "wu_t": np.ascontiguousarray(np.asarray(ins["L0_ffn_up_w"]).T),
        "wd_t": np.ascontiguousarray(np.asarray(ins["L0_ffn_down_w"]).T),
        "wo_t": np.ascontiguousarray(np.asarray(ins["L0_attn_o_proj_w"]).T),
        "w_nat": np.ascontiguousarray(
            np.stack([w0, w1, w2], 0).transpose(2, 0, 1, 3)
            .reshape(128, 3 * H * 128)),
        "w02t": np.ascontiguousarray(
            np.concatenate([w0.transpose(0, 2, 1), w2.transpose(0, 2, 1)], 2)
            .transpose(1, 0, 2).reshape(128, H * 256)),
        "gate_s_r": np.ascontiguousarray(
            np.asarray(ins["L0_ffn_gate_s"]).reshape(NI, 128).T),
        "up_s_r": np.ascontiguousarray(
            np.asarray(ins["L0_ffn_up_s"]).reshape(NI, 128).T),
        "down_s_r": np.ascontiguousarray(
            np.asarray(ins["L0_ffn_down_s"]).reshape(16, 128).T),
        "onorm_r": np.ascontiguousarray(
            np.asarray(ins["L0_attn_o_norm_w"]).reshape(1, 128)),
    }


def kernel(**inputs):
    x = np.asarray(inputs["L0_attn_x"], dtype=np.float32)
    shared = _prep_shared(inputs)
    in_maps = []
    for c in range(NCORES):
        xc = x[2 * c:2 * c + 2]
        m = dict(shared)
        m["x_tok"] = np.ascontiguousarray(xc.reshape(T, D))
        m["x_feat"] = np.ascontiguousarray(
            np.concatenate([xc[0].T, xc[1].T], axis=1))
        in_maps.append(m)
    nc = _get_nc()
    res = run_bass_kernel_spmd(nc, in_maps, core_ids=list(range(NCORES)))
    out = np.empty((16, SEQ, D), dtype=np.float32)
    for c in range(NCORES):
        of = res.results[c]["out_feat"]
        out[2 * c] = of[:, 0:SEQ].T
        out[2 * c + 1] = of[:, SEQ:2 * SEQ].T
    return out


# version 12
# speedup vs baseline: 1.0186x; 1.0186x over previous
import numpy as np
import ml_dtypes
import concourse.bass as bass
import concourse.mybir as mybir
from concourse.bass_utils import run_bass_kernel_spmd

F32 = mybir.dt.float32
BF16 = mybir.dt.bfloat16
AF = mybir.ActivationFunctionType
ALU = mybir.AluOpType
AX = mybir.AxisListType
f32 = np.float32

NCORES = 8
D = 2048
SEQ = 2048
T = 4096
H = 16
NQKV = 6144
NLR = 48
INTER = 5632
NI = INTER // 128
NTC = T // 128
NS_COEFS = [(4.0848, -6.8946, 2.927), (3.9505, -6.3029, 2.6377),
            (3.7418, -5.5913, 2.3037), (2.8769, -3.1427, 1.2046),
            (2.8366, -3.0525, 1.2012)]
BASE_LR_INV = 0.01 + float(np.log(-np.expm1(-0.01)))

STAGE = 6
DEBUG_TAPS = False
P4_HM = 2 * H
P4_SECT = 9

ENGS = ("sync", "gpsimd", "tensor", "vector", "scalar")


NLANES = 10


class Buf:
    __slots__ = ("ws", "r", "excl")
    def __init__(self, fence=(), excl=False):
        self.ws = []
        self.r = list(fence)
        self.excl = excl


class Sched:
    def __init__(self, nc):
        self.nc = nc
        self.ops = []
        self.dma_cnt = {}

    def cut(self):
        last = {}
        for op in self.ops:
            if op["kind"] == "d":
                last[(op["eng"], "d", op["lane"])] = op["i"]
            else:
                last[(op["eng"], "c")] = op["i"]
        return sorted(last.values())

    def add(self, eng, kind, fn, reads=(), writes=(), deps=()):
        i = len(self.ops)
        dep = set()
        waw = set()
        for d in deps:
            if d is not None:
                assert d < i
                dep.add(d)
        writes = list(writes) + [b for b in reads if b.excl]
        reads = [b for b in reads if not b.excl]
        for b in reads:
            dep.update(b.ws)
        for b in writes:
            waw.update(b.ws)
            dep.update(b.r)
        dep -= waw
        op = dict(i=i, eng=eng, kind=kind, fn=fn,
                  deps=sorted(dep), waw=sorted(waw),
                  inc=False, idx=None, lane=None, lidx=None)
        if kind == "d":
            c = self.dma_cnt.get(eng, 0)
            self.dma_cnt[eng] = c + 1
            op["lane"] = c % NLANES
            op["lidx"] = c // NLANES + 1
        self.ops.append(op)
        for b in reads:
            b.r.append(i)
        for b in writes:
            b.ws.append(i)
        return i

    def c(self, eng, fn, reads=(), writes=(), deps=()):
        return self.add(eng, "c", fn, reads, writes, deps)

    def d(self, q, fn, reads=(), writes=(), deps=()):
        return self.add(q, "d", fn, reads, writes, deps)

    @staticmethod
    def _skip(p, op, is_waw):
        if p["eng"] != op["eng"]:
            return False
        if p["kind"] == "c" and op["kind"] == "c":
            return p["eng"] == "tensor"
        if is_waw and p["kind"] == "d" and op["kind"] == "d":
            return True
        return False

    @staticmethod
    def _key(p):
        if p["kind"] == "d":
            return (p["eng"], "d", p["lane"])
        return (p["eng"], "c")

    def emit(self, final_deps):
        nc = self.nc
        self.c("sync", lambda e: e.nop(), deps=final_deps)
        for op in self.ops:
            if op["kind"] == "d":
                op["inc"] = True
            for d in op["deps"]:
                p = self.ops[d]
                if not self._skip(p, op, False):
                    p["inc"] = True
            for d in op["waw"]:
                p = self.ops[d]
                if not self._skip(p, op, True):
                    p["inc"] = True
        counters = {}
        for op in self.ops:
            if op["kind"] == "d":
                op["idx"] = op["lidx"]
            elif op["inc"]:
                key = self._key(op)
                counters[key] = counters.get(key, 0) + 1
                op["idx"] = counters[key]
        keys = set(counters)
        for op in self.ops:
            if op["kind"] == "d":
                keys.add(self._key(op))
        sem_of = {key: nc.alloc_semaphore("s_" + "_".join(map(str, key)))
                  for key in sorted(keys)}
        for op in self.ops:
            waits = {}
            for dlist, is_waw in ((op["deps"], False), (op["waw"], True)):
                for d in dlist:
                    p = self.ops[d]
                    if self._skip(p, op, is_waw):
                        continue
                    key = self._key(p)
                    amt = 16 if p["kind"] == "d" else 1
                    v = p["idx"] * amt
                    if waits.get(key, 0) < v:
                        waits[key] = v
            if op["kind"] == "d" and op["lidx"] > 1:
                key = self._key(op)
                v = (op["lidx"] - 1) * 16
                if waits.get(key, 0) < v:
                    waits[key] = v
            op["waits"] = waits
        per_eng = {e: [] for e in ENGS}
        for op in self.ops:
            per_eng[op["eng"]].append(op)

        with nc.Block() as blk:
            for eng_name in ENGS:
                ops_list = per_eng[eng_name]
                if not ops_list:
                    continue

                def body(e, ops_list=ops_list):
                    hw = {}
                    for op in ops_list:
                        for key in sorted(op["waits"]):
                            v = op["waits"][key]
                            if hw.get(key, 0) >= v:
                                continue
                            e.wait_ge(sem_of[key], v)
                            hw[key] = v
                        inst = op["fn"](e)
                        if op["kind"] == "d":
                            inst.then_inc(sem_of[self._key(op)], 16)
                        elif op["inc"]:
                            inst.then_inc(sem_of[self._key(op)], 1)

                getattr(blk, eng_name)(body)


class Arena:
    def __init__(self, sched, tensor, elems):
        self.S = sched
        self.t = tensor
        self.elems = elems
        self.cur = 0
        self.fence = ()

    def take(self, n):
        assert self.cur + n <= self.elems, (self.cur, n, self.elems)
        off = self.cur
        self.cur += n
        return self.t[:, off:off + n]

    def buf(self):
        return Buf(self.fence)

    def reset(self):
        self.cur = 0
        self.fence = tuple(self.S.cut())


def build_nc():
    nc = bass.Bass("TRN2", target_bir_lowering=False)

    def I(name, shape):
        return nc.dram_tensor(name, list(shape), F32, kind="ExternalInput").ap()

    x_tok = I("x_tok", (T, D))
    x_feat = I("x_feat", (D, T))
    wqkv_t = I("wqkv_t", (D, NQKV + NLR))
    wg_t = I("wg_t", (D, INTER))
    wu_t = I("wu_t", (D, INTER))
    wd_t = I("wd_t", (INTER, D))
    wo_t = I("wo_t", (D, D))
    w_nat = I("w_nat", (128, 3 * H * 128))
    w02t = I("w02t", (128, H * 256))
    gate_s_r = I("gate_s_r", (128, NI))
    up_s_r = I("up_s_r", (128, NI))
    down_s_r = I("down_s_r", (128, 16))
    onorm_r = I("onorm_r", (1, 128))

    out_feat = nc.dram_tensor("out_feat", [D, T], F32, kind="ExternalOutput").ap()

    sk = "ExternalOutput" if DEBUG_TAPS else "Internal"
    xn_h = nc.dram_tensor("xn_h", [16, T, 128], BF16, kind=sk).ap()
    qT_h = nc.dram_tensor("qT_h", [2 * H, SEQ, 128], BF16, kind=sk).ap()
    kT_h = nc.dram_tensor("kT_h", [2 * H, SEQ, 128], BF16, kind=sk).ap()
    vT_h = nc.dram_tensor("vT_h", [2 * H, SEQ, 128], BF16, kind=sk).ap()
    hdn_h = nc.dram_tensor("hdn_h", [INTER, T], BF16, kind=sk).ap()
    wd_bf_h = nc.dram_tensor("wd_bf_h", [INTER, D], BF16, kind="Internal").ap()
    onrm_h = nc.dram_tensor("onrm_h", [2 * H, SEQ, 128], BF16, kind=sk).ap()
    part_h = nc.dram_tensor("part_h", [D, T], F32, kind=sk).ap()

    qT4 = qT_h.rearrange("g (u p) c -> p g u c", p=128)
    kT4 = kT_h.rearrange("g (u p) c -> p g u c", p=128)
    vT4 = vT_h.rearrange("g (u p) c -> p g u c", p=128)
    onrm4 = onrm_h.rearrange("g (u p) c -> p g u c", p=128)
    xn_tok3 = xn_h.rearrange("a (u p) c -> p u a c", p=128)
    hdn3 = hdn_h.rearrange("(a p) t -> p a t", p=128)
    wd3 = wd_t.rearrange("(a p) c -> p a c", p=128)
    wdbf3 = wd_bf_h.rearrange("(a p) c -> p a c", p=128)
    wqkv3 = wqkv_t.rearrange("(a p) o -> p a o", p=128)
    wg3 = wg_t.rearrange("(a p) o -> p a o", p=128)
    wu3 = wu_t.rearrange("(a p) o -> p a o", p=128)
    wo3 = wo_t.rearrange("(a p) o -> p a o", p=128)
    wnat3 = w_nat.rearrange("p (j h c) -> p j h c", j=3, h=H)
    w02t3 = w02t.rearrange("p (h c) -> p h c", h=H)

    A = nc.alloc_sbuf_tensor
    ident = A("ident", [128, 128], BF16)
    gam = A("gam", [128, 128], BF16)
    aI = A("aI", [128, 5 * 128], BF16)
    onesrow = A("onesrow", [1, 128], BF16)
    onescol = A("onescol", [128, 1], BF16)
    onorm_bf = A("onorm_bf", [1, 128], BF16)
    gate_s_sb = A("gate_s_sb", [128, NI], F32)
    up_s_sb = A("up_s_sb", [128, NI], F32)
    down_s_sb = A("down_s_sb", [128, 16], F32)
    lr_sb = A("lr_sb", [128, NTC * NLR], F32)
    lr3 = lr_sb[:, :].rearrange("p (u c) -> p u c", c=NLR)
    ssq_q = A("ssq_q", [128, NTC * H], F32)
    ssq_k = A("ssq_k", [128, NTC * H], F32)
    rq_sb = A("rq_sb", [128, NTC * H], F32)
    rk_sb = A("rk_sb", [128, NTC * H], F32)
    ssq_q3 = ssq_q[:, :].rearrange("p (u h) -> p u h", h=H)
    ssq_k3 = ssq_k[:, :].rearrange("p (u h) -> p u h", h=H)
    rq3 = rq_sb[:, :].rearrange("p (u h) -> p u h", h=H)
    rk3 = rk_sb[:, :].rearrange("p (u h) -> p u h", h=H)
    w02t_sb = A("w02t_sb", [128, H * 256], BF16)
    w02t_sb3 = w02t_sb[:, :].rearrange("p (h c) -> p h c", h=H)
    w1n_sb = A("w1n_sb", [128, H * 128], BF16)
    w1n_sb3 = w1n_sb[:, :].rearrange("p (h c) -> p h c", h=H)
    wnorm_sb = A("wnorm_sb", [128, 3 * H], F32)

    AR1_E = 79232
    AR2_E = 6 * 1024
    ar1_t = A("ar1", [128, AR1_E], BF16)
    ar2_t = A("ar2", [128, AR2_E], F32)

    pbig = [nc.alloc_psum_tensor(f"pbig{i}", [128, 1536], F32) for i in range(2)]
    pb = [pbig[i // 3][:, (i % 3) * 512:(i % 3) * 512 + 512] for i in range(6)]
    pb.append(nc.alloc_psum_tensor("pb6", [128, 512], F32))
    pbt = nc.alloc_psum_tensor("pbt", [128, 1024], BF16)

    S = Sched(nc)
    ar1 = Arena(S, ar1_t, AR1_E)
    ar2 = Arena(S, ar2_t, AR2_E)

    B = Buf
    b_ident, b_gam, b_aI = B(), B(), B()
    b_onesrow, b_onescol, b_onorm = B(), B(), B()
    b_scales = B()
    b_lr, b_ssq_q, b_ssq_k, b_rq, b_rk = B(), B(), B(), B(), B()
    b_w02t, b_w1n, b_wnorm = B(), B(), B()
    b_pb = [B(excl=True) for _ in range(7)]
    b_pbt = B(excl=True)
    b_xn_h = [B(), B()]
    b_q_h = [B(), B()]
    b_kv_h = [B(), B()]
    b_hdn_h = [B(), B()]
    b_onrm_h = [B(), B()]
    b_part = B()
    b_wdbf = B()
    b_out = B()
    out_dmas = []

    S.c("gpsimd", lambda e: e.memset(ident[:, :], 0.0), writes=[b_ident])
    S.c("gpsimd", lambda e: e.affine_select(
        out=ident[:, :], in_=ident[:, :], compare_op=ALU.not_equal,
        fill=1.0, base=0, pattern=[[-1, 128]], channel_multiplier=1),
        writes=[b_ident])
    S.c("gpsimd", lambda e: e.memset(aI[:, :], 0.0), writes=[b_aI])
    for k, (a_k, _, _) in enumerate(NS_COEFS):
        S.c("gpsimd", lambda e, k=k, a_k=a_k: e.affine_select(
            out=aI[:, k * 128:(k + 1) * 128], in_=aI[:, k * 128:(k + 1) * 128],
            compare_op=ALU.not_equal, fill=float(a_k), base=0,
            pattern=[[-1, 128]], channel_multiplier=1), writes=[b_aI])
    S.c("vector", lambda e: e.memset(onesrow[:, :], 1.0), writes=[b_onesrow])
    S.c("vector", lambda e: e.memset(onescol[:, :], 1.0), writes=[b_onescol])
    S.d("sync", lambda e: e.dma_start(gate_s_sb[:, :], gate_s_r[:, :]), writes=[b_scales])
    S.d("sync", lambda e: e.dma_start(up_s_sb[:, :], up_s_r[:, :]), writes=[b_scales])
    S.d("sync", lambda e: e.dma_start(down_s_sb[:, :], down_s_r[:, :]), writes=[b_scales])
    S.d("gpsimd", lambda e: e.dma_start(onorm_bf[:, :], onorm_r[:, :]), writes=[b_onorm])
    S.d("gpsimd", lambda e: e.dma_start(w02t_sb[:, :], w02t[:, :]), writes=[b_w02t])
    S.d("gpsimd", lambda e: e.dma_start(w1n_sb[:, :], w_nat[:, 2048:4096]), writes=[b_w1n])
    S.c("tensor", lambda e: e.matmul(pb[6][:, 0:128], onesrow[:, :], onorm_bf[:, :],
                                     start=True, stop=True),
        reads=[b_onesrow, b_onorm], writes=[b_pb[6]])
    S.c("vector", lambda e: e.tensor_copy(gam[:, :], pb[6][:, 0:128]),
        reads=[b_pb[6]], writes=[b_gam])
    wtmp = ar2.take(6144)
    wtmp3 = wtmp.rearrange("p (j h c) -> p j h c", j=3, h=H)
    b_wtmp = ar2.buf()
    junk = ar1.take(128)
    b_junk = ar1.buf()
    S.d("sync", lambda e: e.dma_start(wtmp, w_nat[:, :]), writes=[b_wtmp])
    for j in range(3):
        for h in range(H):
            S.c("vector", lambda e, j=j, h=h: e.scalar_tensor_tensor(
                junk, wtmp3[:, j, h, :], 1.0, wtmp3[:, j, h, :],
                ALU.mult, ALU.mult,
                accum_out=wnorm_sb[:, j * H + h: j * H + h + 1]),
                reads=[b_wtmp], writes=[b_wnorm, b_junk])
    S.c("scalar", lambda e: e.activation(wnorm_sb[:, :], wnorm_sb[:, :], AF.Sqrt),
        reads=[b_wnorm], writes=[b_wnorm])
    ar1.reset()
    ar2.reset()

    if STAGE >= 1:
        xw = [ar2.take(2048) for _ in range(2)] + \
            [ar1.take(4096).bitcast(F32) for _ in range(2)]
        b_xw = [ar2.buf(), ar2.buf(), ar1.buf(), ar1.buf()]
        st_t = ar2.take(NTC * 8)
        st3 = st_t.rearrange("p (u c) -> p u c", c=8)
        b_st = ar2.buf()
        sq_scr = ar1.take(2048)
        b_sq = ar1.buf()
        xo = [ar1.take(2048), ar1.take(2048)]
        b_xo = [ar1.buf(), ar1.buf()]
        for i in range(NTC):
            w = i % 4
            half = i // 16
            S.d("sync" if i % 2 == 0 else "gpsimd",
                lambda e, i=i, w=w: e.dma_start(
                    xw[w], x_tok[i * 128:(i + 1) * 128, :]), writes=[b_xw[w]])
            S.c("vector", lambda e, i=i, w=w: e.reduce_sum(
                st3[:, i, 0:1], xw[w], AX.X), reads=[b_xw[w]], writes=[b_st])
            S.c("scalar", lambda e, i=i, w=w: e.activation(
                sq_scr, xw[w], AF.Square, accum_out=st3[:, i, 1:2]),
                reads=[b_xw[w]], writes=[b_st, b_sq])
            S.c("vector", lambda e, i=i: e.tensor_scalar(
                st3[:, i, 2:3], st3[:, i, 0:1], float(-1.0 / D), None, ALU.mult),
                reads=[b_st], writes=[b_st])
            S.c("vector", lambda e, i=i: e.scalar_tensor_tensor(
                st3[:, i, 3:4], st3[:, i, 0:1], st3[:, i, 2:3], st3[:, i, 1:2],
                ALU.mult, ALU.add), reads=[b_st], writes=[b_st])
            S.c("scalar", lambda e, i=i: e.activation(
                st3[:, i, 4:5], st3[:, i, 3:4], AF.Sqrt,
                scale=f32(1.0 / D), bias=f32(1e-5)), reads=[b_st], writes=[b_st])
            S.c("vector", lambda e, i=i: e.reciprocal(
                st3[:, i, 5:6], st3[:, i, 4:5]), reads=[b_st], writes=[b_st])
            S.c("vector", lambda e, i=i: e.tensor_mul(
                st3[:, i, 6:7], st3[:, i, 2:3], st3[:, i, 5:6]),
                reads=[b_st], writes=[b_st])
            S.c("scalar", lambda e, i=i, w=w: e.activation(
                xo[w % 2], xw[w], AF.Identity,
                bias=st3[:, i, 6:7], scale=st3[:, i, 5:6]),
                reads=[b_xw[w], b_st], writes=[b_xo[w % 2]])
            S.d("sync", lambda e, i=i, w=w: e.dma_start(
                xn_tok3[:, i, :, :],
                xo[w % 2].rearrange("p (a c) -> p a c", a=16)),
                reads=[b_xo[w % 2]], writes=[b_xn_h[half]])
        ar1.reset()
        ar2.reset()

    QOFB = 4
    NOFB = 12
    if STAGE >= 2:
        xn_t = ar1.take(16 * 2048)
        xn3 = xn_t.rearrange("p (a t) -> p a t", a=16)
        b_xn = ar1.buf()
        wq_buf = [ar1.take(16 * 512), ar1.take(16 * 512)]
        wq3 = [t.rearrange("p (a o) -> p a o", a=16) for t in wq_buf]
        b_wq = [ar1.buf(), ar1.buf()]
        ev_st = [ar1.take(512) for _ in range(3)]
        b_ev = [ar1.buf() for _ in range(3)]
        sqt = [ar1.take(512) for _ in range(2)]
        b_sqt = [ar1.buf() for _ in range(2)]
        lr_tmp = [ar2.take(NLR) for _ in range(3)]
        b_lrt = [ar2.buf() for _ in range(3)]
        r_tmp = ar2.take(256)
        b_rtmp = ar2.buf()
        wgu_buf = [ar1.take(16 * 128) for _ in range(4)]
        wgu3 = [t.rearrange("p (a o) -> p a o", a=16) for t in wgu_buf]
        b_wgu = [ar1.buf() for _ in range(4)]
        gtmp = [ar1.take(512) for _ in range(4)]
        b_gt = [ar1.buf() for _ in range(4)]
        htmp = [ar1.take(512) for _ in range(4)]
        b_ht = [ar1.buf() for _ in range(4)]
        wdc = [ar1.take(2048), ar1.take(2048)]
        wdc3 = [t.rearrange("p (a c) -> p a c", a=1) for t in wdc]
        b_wdc = [ar1.buf(), ar1.buf()]
        qp = [ar1.take(2048), ar1.take(2048)]
        qp3 = [t.rearrange("p (h c) -> p h c", h=16) for t in qp]
        b_qp = [ar1.buf(), ar1.buf()]

    def p3_half(half):
        for a in range(16):
            S.d("sync", lambda e, a=a, half=half: e.dma_start(
                xn3[:, a, :],
                xn_h[a, half * 2048:(half + 1) * 2048, :],
                transpose=True),
                reads=[b_xn_h[half]], writes=[b_xn])
        for ofb in range(NOFB + 1):
            wslot = ofb % 2
            is_lr = (ofb == NOFB)
            width = NLR if is_lr else 512
            S.d("gpsimd", lambda e, ofb=ofb, wslot=wslot, width=width: e.dma_start(
                wq3[wslot][:, :, 0:width],
                wqkv3[:, :, ofb * 512: ofb * 512 + width]),
                writes=[b_wq[wslot]])
            for tc in range(16):
                u = half * 16 + tc
                bank = tc % 3
                for a in range(16):
                    S.c("tensor", lambda e, a=a, tc=tc, wslot=wslot, width=width, bank=bank:
                        e.matmul(pb[bank][:, 0:width],
                                 xn3[:, a, tc * 128:(tc + 1) * 128],
                                 wq3[wslot][:, a, 0:width],
                                 start=(a == 0), stop=(a == 15)),
                        reads=[b_xn, b_wq[wslot]], writes=[b_pb[bank]])
                if is_lr:
                    lt = tc % 3
                    S.c("scalar", lambda e, bank=bank, lt=lt: e.activation(
                        lr_tmp[lt], pb[bank][:, 0:NLR], AF.Abs,
                        bias=f32(BASE_LR_INV)),
                        reads=[b_pb[bank]], writes=[b_lrt[lt]])
                    S.c("scalar", lambda e, lt=lt: e.activation(
                        lr_tmp[lt], lr_tmp[lt], AF.Exp, scale=f32(-1.0)),
                        reads=[b_lrt[lt]], writes=[b_lrt[lt]])
                    S.c("scalar", lambda e, lt=lt: e.activation(
                        lr_tmp[lt], lr_tmp[lt], AF.Ln, bias=f32(1.0)),
                        reads=[b_lrt[lt]], writes=[b_lrt[lt]])
                    S.c("scalar", lambda e, bank=bank, u=u: e.activation(
                        lr3[:, u, :], pb[bank][:, 0:NLR], AF.Relu,
                        bias=f32(BASE_LR_INV)),
                        reads=[b_pb[bank]], writes=[b_lr])
                    S.c("vector", lambda e, u=u, lt=lt: e.tensor_add(
                        lr3[:, u, :], lr3[:, u, :], lr_tmp[lt]),
                        reads=[b_lrt[lt]], writes=[b_lr])
                else:
                    es = tc % 3
                    S.c("scalar", lambda e, bank=bank, es=es: e.activation(
                        ev_st[es], pb[bank][:, 0:512], AF.Silu),
                        reads=[b_pb[bank]], writes=[b_ev[es]])
                    hbuf = b_q_h[half] if ofb < QOFB else b_kv_h[half]
                    dstT = (qT4 if ofb < QOFB else
                            (kT4 if ofb < 2 * QOFB else vT4))
                    g0 = half * 16 + (ofb % QOFB) * 4
                    S.d("sync", lambda e, u=u, tc=tc, g0=g0, es=es, dstT=dstT:
                        e.dma_start(
                            dstT[:, g0:g0 + 4, tc, :],
                            ev_st[es].rearrange("p (g c) -> p g c", g=4)),
                        reads=[b_ev[es]], writes=[hbuf])
                    if ofb < 8:
                        ssq3 = ssq_q3 if ofb < QOFB else ssq_k3
                        bs = b_ssq_q if ofb < QOFB else b_ssq_k
                        h0 = (ofb % 4) * 4
                        qs = tc % 2
                        ev3 = ev_st[es].rearrange("p (a b) -> p a b", b=128)
                        sq3 = sqt[qs].rearrange("p (a b) -> p a b", b=128)
                        S.c("vector", lambda e, es=es, qs=qs, ev3=ev3, sq3=sq3:
                            e.tensor_mul(sq3, ev3, ev3),
                            reads=[b_ev[es]], writes=[b_sqt[qs]])
                        S.c("vector", lambda e, qs=qs, sq3=sq3, u=u, h0=h0, ssq3=ssq3:
                            e.tensor_reduce(ssq3[:, u, h0:h0 + 4], sq3, AX.X, ALU.add),
                            reads=[b_sqt[qs]], writes=[bs])
        for (ssq3v, bs, r3v, br) in ((ssq_q3, b_ssq_q, rq3, b_rq),
                                     (ssq_k3, b_ssq_k, rk3, b_rk)):
            sl = slice(half * 16, (half + 1) * 16)
            S.c("scalar", lambda e, ssq3v=ssq3v, sl=sl, bs=bs: e.activation(
                r_tmp, ssq3v[:, sl, :].rearrange("p a b -> p (a b)"), AF.Ln,
                bias=f32(1e-30)), reads=[bs], writes=[b_rtmp])
            S.c("scalar", lambda e: e.activation(
                r_tmp, r_tmp, AF.Exp, scale=f32(0.5)),
                reads=[b_rtmp], writes=[b_rtmp])
            S.c("vector", lambda e: e.tensor_scalar(
                r_tmp, r_tmp, float(1e-5), None, ALU.add),
                reads=[b_rtmp], writes=[b_rtmp])
            S.c("vector", lambda e, r3v=r3v, sl=sl, br=br: e.reciprocal(
                r3v[:, sl, :].rearrange("p a b -> p (a b)"), r_tmp),
                reads=[b_rtmp], writes=[br])
        for tc in range(16):
            u = half * 16 + tc
            w = tc % 2
            g0 = half * 16
            S.d("sync", lambda e, tc=tc, w=w, g0=g0: e.dma_start(
                qp3[w], qT4[:, g0:g0 + 16, tc, :]),
                reads=[b_q_h[half]], writes=[b_qp[w]])
            S.c("vector", lambda e, u=u, w=w: e.tensor_mul(
                qp3[w], qp3[w],
                rq3[:, u, :][:, :, None].broadcast_to([128, 16, 128])),
                reads=[b_rq, b_qp[w]], writes=[b_qp[w]])
            S.d("sync", lambda e, tc=tc, w=w, g0=g0: e.dma_start(
                qT4[:, g0:g0 + 16, tc, :], qp3[w]),
                reads=[b_qp[w]], writes=[b_q_h[half]])

    def p3b_half(half):
        bank_map = [[0, 1, 2, 3], [4, 5, 6, 3]]
        for of in range(NI):
            sg = of % 2
            S.d("gpsimd", lambda e, of=of, sg=sg: e.dma_start(
                wgu3[2 * sg], wg3[:, :, of * 128:(of + 1) * 128]),
                writes=[b_wgu[2 * sg]])
            S.d("gpsimd", lambda e, of=of, sg=sg: e.dma_start(
                wgu3[2 * sg + 1], wu3[:, :, of * 128:(of + 1) * 128]),
                writes=[b_wgu[2 * sg + 1]])
            for tq in range(2):
                bm = bank_map[tq]
                for gi in range(2):
                    wb = b_wgu[2 * sg + gi]
                    wv = wgu3[2 * sg + gi]
                    for a in range(16):
                        for t in range(2):
                            bank = bm[gi * 2 + t]
                            S.c("tensor", lambda e, a=a, t=t, tq=tq, wv=wv, bank=bank:
                                e.matmul(pb[bank][:, 0:512], wv[:, a, :],
                                         xn3[:, a, tq * 1024 + t * 512:
                                             tq * 1024 + (t + 1) * 512],
                                         start=(a == 0), stop=(a == 15)),
                                reads=[b_xn, wb], writes=[b_pb[bank]])
                for t in range(2):
                    es = 2 * tq + t
                    gb, ub = bm[t], bm[2 + t]
                    S.c("scalar", lambda e, of=of, es=es, gb=gb: e.activation(
                        gtmp[es], pb[gb][:, 0:512], AF.Silu,
                        scale=gate_s_sb[:, of:of + 1]),
                        reads=[b_pb[gb], b_scales], writes=[b_gt[es]])
                    S.c("vector", lambda e, of=of, es=es, ub=ub: e.scalar_tensor_tensor(
                        htmp[es], pb[ub][:, 0:512], up_s_sb[:, of:of + 1],
                        gtmp[es], ALU.mult, ALU.mult),
                        reads=[b_pb[ub], b_gt[es], b_scales], writes=[b_ht[es]])
                    off = half * 2048 + tq * 1024 + t * 512
                    S.d("sync", lambda e, of=of, off=off, es=es: e.dma_start(
                        hdn3[:, of, off:off + 512], htmp[es]),
                        reads=[b_ht[es]], writes=[b_hdn_h[half]])
        if half == 0:
            for cc in range(NI):
                w = cc % 2
                S.d("gpsimd", lambda e, cc=cc, w=w: e.dma_start(
                    wdc3[w], wd3[:, cc:cc + 1, :]), writes=[b_wdc[w]])
                S.d("sync", lambda e, cc=cc, w=w: e.dma_start(
                    wdbf3[:, cc:cc + 1, :], wdc3[w]),
                    reads=[b_wdc[w]], writes=[b_wdbf])

    if STAGE >= 2:
        p3_half(0)
        if STAGE >= 3:
            p3b_half(0)
        p3_half(1)
        if STAGE >= 3:
            p3b_half(1)
        ar1.reset()
        ar2.reset()

    if STAGE >= 4:
        NMAT = 3 * P4_HM
        DWS_t = ar1.take(96 * 128)
        DWS3 = DWS_t.rearrange("p (n c) -> p n c", n=96)
        b_dw = [ar1.buf() for _ in range(96)]
        xts_off = ar1.cur
        XTS_t = ar1.take(96 * 128)
        XTS3 = XTS_t.rearrange("p (n c) -> p n c", n=96)
        b_xt96 = [ar1.buf() for _ in range(96)]
        csb96 = ar1.take(128)
        b_csb = ar1.buf()
        cs96 = ar2.take(96)
        b_cs = ar2.buf()
        scl96 = ar2.take(96)
        b_scl = ar2.buf()
        ar1_mark = ar1.cur
        ar2_mark = ar2.cur

        par = []
        for p in range(2):
            sl = {}
            for name in ("KT", "VT", "SG", "DS", "HID", "DGA", "DB2"):
                sl[name] = ar1.take(2048)
            sl["GH"] = ar1.take(16 * 384)
            sm = {"lrk": ar2.take(32)}
            bufs = {k: ar1.buf() for k in list(sl) + ["lrk"]}
            bufs["DH"] = bufs["GH"]
            par.append((sl, sm, bufs))
        kfr = [ar1.take(2048) for _ in range(3)]
        vfr = [ar1.take(2048) for _ in range(3)]
        b_kfr = [ar1.buf() for _ in range(3)]
        b_vfr = [ar1.buf() for _ in range(3)]

        def p4a_streams(hm):
            half = hm // H
            h = hm % H
            hh = half * 16 + h
            r = hm % 3
            S.d("sync", lambda e, hh=hh, r=r: e.dma_start(
                kfr[r], kT_h[hh], transpose=True),
                reads=[b_kv_h[half]], writes=[b_kfr[r]])
            S.d("sync", lambda e, hh=hh, r=r: e.dma_start(
                vfr[r], vT_h[hh], transpose=True),
                reads=[b_kv_h[half]], writes=[b_vfr[r]])

        def p4a(hm, dwg=None):
            half = hm // H
            h = hm % H
            p = hm % 2
            r = hm % 3
            sl, sm, bb = par[p]
            bb = dict(bb)
            bb["KF"] = b_kfr[r]
            bb["VF"] = b_vfr[r]
            bank = [b_pb[3 * p + i] for i in range(3)]
            pbk = [pb[3 * p + i] for i in range(3)]
            u0 = half * 16
            KF, VF, KT, VT = kfr[r], vfr[r], sl["KT"], sl["VT"]
            GH3 = sl["GH"].rearrange("p (a c) -> p a c", a=16)
            gba = GH3[:, :, 0:128]
            hbm = GH3[:, :, 128:256]
            gbahbm = GH3[:, :, 0:256]
            DH3 = GH3[:, :, 256:384]
            KF3 = KF.rearrange("p (a c) -> p a c", a=16)
            VF3 = VF.rearrange("p (a c) -> p a c", a=16)
            KT3 = KT.rearrange("p (a c) -> p a c", a=16)
            VT3 = VT.rearrange("p (a c) -> p a c", a=16)
            SG3 = sl["SG"].rearrange("p (a c) -> p a c", a=16)
            DS3 = sl["DS"].rearrange("p (a c) -> p a c", a=16)
            HID3 = sl["HID"].rearrange("p (a c) -> p a c", a=16)
            DGA3 = sl["DGA"].rearrange("p (a c) -> p a c", a=16)
            DB23 = sl["DB2"].rearrange("p (a c) -> p a c", a=16)

            hh = half * 16 + h
            S.d("sync", lambda e, hh=hh: e.dma_start(
                KT3, kT4[:, hh, :, :]),
                reads=[b_kv_h[half]], writes=[bb["KT"]])
            S.d("sync", lambda e, hh=hh: e.dma_start(
                VT3, vT4[:, hh, :, :]),
                reads=[b_kv_h[half]], writes=[bb["VT"]])

            pgrp = pbig[p]
            for w0sc in range(0, 16, 3):
                ns = min(3, 16 - w0sc)
                for i in range(ns):
                    sc = w0sc + i
                    pk = pbk[i]
                    S.c("tensor", lambda e, sc=sc, pk=pk: e.matmul(
                        pk[:, 0:256], KF3[:, sc, :], w02t_sb3[:, h, :],
                        start=True, stop=True),
                        reads=[bb["KF"], b_w02t], writes=[bank[i]])
                    S.c("tensor", lambda e, sc=sc, pk=pk: e.matmul(
                        pk[:, 256:384], VF3[:, sc, :], w1n_sb3[:, h, :],
                        start=True, stop=True),
                        reads=[bb["VF"], b_w1n], writes=[bank[i]])
                span = pgrp[:, 0:3 * 512].rearrange(
                    "p (b c) -> p b c", b=3)[:, 0:ns, 0:384]
                S.c("scalar", lambda e, w0sc=w0sc, ns=ns, span=span: e.activation(
                    GH3[:, w0sc:w0sc + ns, :], span, AF.Identity),
                    reads=[], writes=[bank[0], bank[1], bank[2], bb["GH"]])
                if dwg is not None:
                    next(dwg, None)

            S.c("vector", lambda e: e.tensor_mul(
                gbahbm, gbahbm,
                rk3[:, u0:u0 + 16, h:h + 1].broadcast_to([128, 16, 256])),
                reads=[b_rk], writes=[bb["GH"]])
            S.c("scalar", lambda e: e.activation(DS3, gba, AF.Sigmoid),
                reads=[bb["GH"]], writes=[bb["DS"]])
            S.c("vector", lambda e: e.tensor_mul(SG3, gba, DS3),
                reads=[bb["GH"], bb["DS"]], writes=[bb["SG"]])
            S.c("vector", lambda e: e.tensor_mul(HID3, SG3, hbm),
                reads=[bb["SG"], bb["GH"]], writes=[bb["HID"]])
            S.c("gpsimd", lambda e: e.tensor_mul(DGA3, DH3, hbm),
                reads=[bb["DH"], bb["GH"]], writes=[bb["DGA"]])
            S.c("gpsimd", lambda e: e.tensor_mul(DB23, DH3, SG3),
                reads=[bb["DH"], bb["SG"]], writes=[bb["DB2"]])
            S.c("vector", lambda e: e.tensor_tensor(gba, gba, SG3, ALU.subtract),
                reads=[bb["SG"]], writes=[bb["GH"]])
            S.c("vector", lambda e: e.scalar_tensor_tensor(
                DS3, gba, 1.0, DS3, ALU.add, ALU.mult),
                reads=[bb["GH"]], writes=[bb["DS"]])
            S.c("vector", lambda e: e.tensor_mul(sl["DGA"], sl["DGA"], sl["DS"]),
                reads=[bb["DS"]], writes=[bb["DGA"]])
            lrk3 = sm["lrk"].rearrange("p (a c) -> p a c", a=2)
            S.c("vector", lambda e: e.tensor_mul(
                lrk3[:, 0, :], lr3[:, u0:u0 + 16, 3 * h], rk3[:, u0:u0 + 16, h]),
                reads=[b_lr, b_rk], writes=[bb["lrk"]])
            S.c("vector", lambda e: e.tensor_mul(
                lrk3[:, 1, :], lr3[:, u0:u0 + 16, 3 * h + 2], rk3[:, u0:u0 + 16, h]),
                reads=[b_lr, b_rk], writes=[bb["lrk"]])
            S.c("vector", lambda e: e.tensor_mul(
                KF3, KT3, lrk3[:, 1, :][:, :, None].broadcast_to([128, 16, 128])),
                reads=[bb["KT"], bb["lrk"]], writes=[bb["KF"]])
            S.c("gpsimd", lambda e: e.tensor_mul(
                KT3, KT3, lrk3[:, 0, :][:, :, None].broadcast_to([128, 16, 128])),
                reads=[bb["lrk"]], writes=[bb["KT"]])
            S.c("vector", lambda e: e.tensor_mul(
                HID3, HID3,
                lr3[:, u0:u0 + 16, 3 * h + 1][:, :, None].broadcast_to([128, 16, 128])),
                reads=[b_lr], writes=[bb["HID"]])

        def p4a_dw_gen(hm):
            p = hm % 2
            r = hm % 3
            sl, sm, bb = par[p]
            bb = dict(bb)
            bb["KF"] = b_kfr[r]
            bank = [b_pb[3 * p + i] for i in range(3)]
            pbk = [pb[3 * p + i] for i in range(3)]
            KF3 = kfr[r].rearrange("p (a c) -> p a c", a=16)
            KT3 = sl["KT"].rearrange("p (a c) -> p a c", a=16)
            VT3 = sl["VT"].rearrange("p (a c) -> p a c", a=16)
            HID3 = sl["HID"].rearrange("p (a c) -> p a c", a=16)
            DGA3 = sl["DGA"].rearrange("p (a c) -> p a c", a=16)
            DB23 = sl["DB2"].rearrange("p (a c) -> p a c", a=16)
            ops = ((DGA3, KT3), (VT3, HID3), (DB23, KF3))
            nmA = ("DGA", "VT", "DB2")
            nmB = ("KT", "HID", "KF")
            for j in range(3):
                A3, B3 = ops[j]
                for sc in range(16):
                    S.c("tensor", lambda e, sc=sc, j=j, A3=A3, B3=B3, pbk=pbk:
                        e.matmul(pbk[j][:, 0:128], A3[:, sc, :], B3[:, sc, :],
                                 start=(sc == 0), stop=(sc == 15)),
                        reads=[bb[nmA[j]], bb[nmB[j]]], writes=[bank[j]])
                yield None
                for sc in range(16):
                    S.c("tensor", lambda e, sc=sc, j=j, A3=A3, B3=B3, pbk=pbk:
                        e.matmul(pbk[j][:, 128:256], B3[:, sc, :], A3[:, sc, :],
                                 start=(sc == 0), stop=(sc == 15)),
                        reads=[bb[nmA[j]], bb[nmB[j]]], writes=[bank[j]])
                idx = hm * 3 + j
                S.c("vector", lambda e, j=j, idx=idx, pbk=pbk: e.tensor_copy(
                    DWS3[:, idx, :], pbk[j][:, 0:128]),
                    reads=[], writes=[bank[j], b_dw[idx]])
                S.c("scalar", lambda e, j=j, idx=idx, pbk=pbk: e.activation(
                    XTS3[:, idx, :], pbk[j][:, 128:256], AF.Identity),
                    reads=[], writes=[bank[j], b_xt96[idx]])
                S.c("vector", lambda e, j=j, idx=idx: e.scalar_tensor_tensor(
                    sl["SG"][:, 0:128], DWS3[:, idx, :], 1.0, DWS3[:, idx, :],
                    ALU.mult, ALU.mult, accum_out=cs96[:, idx:idx + 1]),
                    reads=[b_dw[idx]], writes=[b_cs, bb["SG"]])
                yield None

        def p4a_dw(hm):
            for _ in p4a_dw_gen(hm):
                pass

        p4a_streams(0)
        p4a_streams(1)
        for hm in range(P4_HM):
            dwg = p4a_dw_gen(hm - 1) if hm > 0 else None
            p4a(hm, dwg)
            if dwg is not None:
                for _ in dwg:
                    pass
            if hm + 2 < P4_HM:
                p4a_streams(hm + 2)
        p4a_dw(P4_HM - 1)

        fence_b = tuple(S.cut())
        ar1.cur = ar1_mark
        hb_t = ar1.take(NI * 512)
        hb3 = hb_t.rearrange("p (a t) -> p a t", a=NI)
        b_hb = Buf(fence_b)
        wdb = [ar1.take(NI * 128), ar1.take(NI * 128)]
        wdb3 = [t.rearrange("p (a c) -> p a c", a=NI) for t in wdb]
        b_wdb = [Buf(fence_b), Buf(fence_b)]
        ffs = [ar2.take(512), ar2.take(512)]
        b_ffs = [ar2.buf(), ar2.buf()]

        def p6_units():
            for tb in range(8):
                half = tb // 4
                for seg in range(4):
                    S.d("sync", lambda e, tb=tb, seg=seg: e.dma_start(
                        hb3[:, seg * 11:(seg + 1) * 11, :],
                        hdn3[:, seg * 11:(seg + 1) * 11,
                             tb * 512:(tb + 1) * 512]),
                        reads=[b_hdn_h[half]], writes=[b_hb])
                for oc in range(16):
                    ws = oc % 2
                    S.d("gpsimd", lambda e, oc=oc, ws=ws: e.dma_start(
                        wdb3[ws], wdbf3[:, :, oc * 128:(oc + 1) * 128]),
                        reads=[b_wdbf], writes=[b_wdb[ws]])
                    for a in range(NI):
                        S.c("tensor", lambda e, a=a, ws=ws: e.matmul(
                            pb[6][:, 0:512], wdb3[ws][:, a, :],
                            hb3[:, a, :],
                            start=(a == 0), stop=(a == NI - 1)),
                            reads=[b_wdb[ws], b_hb], writes=[b_pb[6]])
                    fs = oc % 2
                    S.c("vector", lambda e, oc=oc, fs=fs:
                        e.tensor_scalar(
                            ffs[fs], pb[6][:, 0:512],
                            down_s_sb[:, oc:oc + 1], None, ALU.mult),
                        reads=[b_scales], writes=[b_pb[6], b_ffs[fs]])
                    tg = tb * 512
                    S.d("sync", lambda e, oc=oc, tg=tg, fs=fs: e.dma_start(
                        part_h[oc * 128:(oc + 1) * 128, tg:tg + 512], ffs[fs]),
                        reads=[b_ffs[fs]], writes=[b_part])
                    yield None

        p6_gen = p6_units() if STAGE >= 5 else iter(())

        def p6_feed(n):
            for _ in range(n):
                try:
                    next(p6_gen)
                except StopIteration:
                    return

        nsb_t = [ar1.take(3 * 128) for _ in range(2)]
        nsB_t = [ar1.take(3 * 128) for _ in range(2)]
        nsb3 = [t.rearrange("p (b c) -> p b c", b=3) for t in nsb_t]
        nsB3 = [t.rearrange("p (b c) -> p b c", b=3) for t in nsB_t]
        b_nsb = [Buf(fence_b), Buf(fence_b)]
        b_nsB = [Buf(fence_b), Buf(fence_b)]
        S.c("vector", lambda e: e.tensor_copy(csb96[:, 0:96], cs96),
            reads=[b_cs], writes=[b_csb])
        S.c("tensor", lambda e: e.matmul(
            pb[6][0:1, 0:96], onescol[:, :], csb96[:, 0:96],
            start=True, stop=True),
            reads=[b_csb, b_onescol], writes=[b_pb[6]])
        S.c("scalar", lambda e: e.activation(
            scl96[0:1, :], pb[6][0:1, 0:96], AF.Sqrt),
            reads=[], writes=[b_pb[6], b_scl])
        S.c("vector", lambda e: e.tensor_scalar(
            scl96[0:1, :], scl96[0:1, :], float(1e-7), None, ALU.add),
            reads=[], writes=[b_scl])
        S.c("vector", lambda e: e.reciprocal(scl96[0:1, :], scl96[0:1, :]),
            reads=[], writes=[b_scl])
        S.c("vector", lambda e: e.tensor_copy(csb96[0:1, 0:96], scl96[0:1, :]),
            reads=[b_scl], writes=[b_csb])
        S.c("tensor", lambda e: e.matmul(
            pb[6][:, 128:224], onesrow[:, :], csb96[0:1, 0:96],
            start=True, stop=True),
            reads=[b_csb, b_onesrow], writes=[b_pb[6]])
        S.c("vector", lambda e: e.tensor_copy(scl96[:, :], pb[6][:, 128:224]),
            reads=[], writes=[b_pb[6], b_scl])
        S.c("vector", lambda e: e.tensor_mul(
            DWS3[:, 0:NMAT, :], DWS3[:, 0:NMAT, :],
            scl96[:, 0:NMAT][:, :, None].broadcast_to([128, NMAT, 128])),
            reads=[b_scl] + b_dw[:NMAT], writes=b_dw[:NMAT])
        S.c("vector", lambda e: e.tensor_mul(
            XTS3[:, 0:NMAT, :], XTS3[:, 0:NMAT, :],
            scl96[:, 0:NMAT][:, :, None].broadcast_to([128, NMAT, 128])),
            reads=[b_scl] + b_xt96[:NMAT], writes=b_xt96[:NMAT])
        for k, (a_k, b_k, c_k) in enumerate(NS_COEFS):
            cb2 = float(c_k / (b_k * b_k))
            for g0 in range(0, NMAT, 6):
                p6_feed(1)
                grp = list(range(g0, min(g0 + 6, NMAT)))
                halves = [[i for i in grp if i % 6 < 3],
                          [i for i in grp if i % 6 >= 3]]
                def bankof(idx):
                    return b_pb[idx % 6], pb[idx % 6]
                for idx in grp:
                    bk, pk = bankof(idx)
                    xtap = XTS3[:, idx, :]
                    S.c("tensor", lambda e, pk=pk, xtap=xtap: e.matmul(
                        pk[:, 0:128], xtap, xtap, start=True, stop=True),
                        reads=[b_xt96[idx]], writes=[bk])
                for hf in range(2):
                    ids = halves[hf]
                    if not ids:
                        continue
                    n = len(ids)
                    span = pbig[hf][:, 0:1536].rearrange(
                        "p (b c) -> p b c", b=3)[:, 0:n, 0:128]
                    S.c("vector", lambda e, hf=hf, n=n, span=span, b_k=b_k:
                        e.tensor_scalar(nsb3[hf][:, 0:n, :], span, float(b_k),
                                        None, ALU.mult),
                        reads=[], writes=[b_pb[3 * hf], b_pb[3 * hf + 1],
                                          b_pb[3 * hf + 2], b_nsb[hf]])
                for idx in grp:
                    bk, pk = bankof(idx)
                    hf = 0 if idx % 6 < 3 else 1
                    bq = idx % 3
                    S.c("tensor", lambda e, pk=pk, hf=hf, bq=bq: e.matmul(
                        pk[:, 128:256], nsb3[hf][:, bq, :], nsb3[hf][:, bq, :],
                        start=True, stop=True),
                        reads=[b_nsb[hf]], writes=[bk])
                for hf in range(2):
                    ids = halves[hf]
                    if not ids:
                        continue
                    n = len(ids)
                    span = pbig[hf][:, 0:1536].rearrange(
                        "p (b c) -> p b c", b=3)[:, 0:n, 128:256]
                    S.c("vector", lambda e, hf=hf, n=n, span=span, cb2=cb2:
                        e.scalar_tensor_tensor(
                            nsB3[hf][:, 0:n, :], span, cb2, nsb3[hf][:, 0:n, :],
                            ALU.mult, ALU.add),
                        reads=[b_nsb[hf]], writes=[b_pb[3 * hf], b_pb[3 * hf + 1],
                                                   b_pb[3 * hf + 2], b_nsB[hf]])
                for idx in grp:
                    bk, pk = bankof(idx)
                    hf = 0 if idx % 6 < 3 else 1
                    bq = idx % 3
                    xap = DWS3[:, idx, :]
                    S.c("tensor", lambda e, pk=pk, hf=hf, bq=bq, xap=xap: e.matmul(
                        pk[:, 256:384], nsB3[hf][:, bq, :], xap,
                        start=True, stop=False),
                        reads=[b_nsB[hf], b_dw[idx]], writes=[bk])
                    S.c("tensor", lambda e, pk=pk, k=k, xap=xap: e.matmul(
                        pk[:, 256:384], aI[:, k * 128:(k + 1) * 128], xap,
                        start=False, stop=True),
                        reads=[b_aI, b_dw[idx]], writes=[bk])
                for idx in grp:
                    bk, pk = bankof(idx)
                    hf = 0 if idx % 6 < 3 else 1
                    bq = idx % 3
                    xap = DWS3[:, idx, :]
                    S.c("tensor", lambda e, pk=pk, hf=hf, bq=bq, xap=xap: e.matmul(
                        pk[:, 384:512], xap, nsB3[hf][:, bq, :],
                        start=True, stop=False),
                        reads=[b_nsB[hf], b_dw[idx]], writes=[bk])
                    S.c("tensor", lambda e, pk=pk, k=k, xap=xap: e.matmul(
                        pk[:, 384:512], xap, aI[:, k * 128:(k + 1) * 128],
                        start=False, stop=True),
                        reads=[b_aI, b_dw[idx]], writes=[bk])
                for hf in range(2):
                    ids = halves[hf]
                    if not ids:
                        continue
                    n = len(ids)
                    lo = ids[0]
                    span = pbig[hf][:, 0:1536].rearrange(
                        "p (b c) -> p b c", b=3)[:, 0:n, 256:384]
                    S.c("scalar", lambda e, span=span, lo=lo, n=n: e.activation(
                        DWS3[:, lo:lo + n, :], span, AF.Identity),
                        reads=[], writes=[b_pb[3 * hf], b_pb[3 * hf + 1],
                                          b_pb[3 * hf + 2]] + b_dw[lo:lo + n])
                    span2 = pbig[hf][:, 0:1536].rearrange(
                        "p (b c) -> p b c", b=3)[:, 0:n, 384:512]
                    S.c("vector", lambda e, span2=span2, lo=lo, n=n: e.tensor_copy(
                        XTS3[:, lo:lo + n, :], span2),
                        reads=[], writes=[b_pb[3 * hf], b_pb[3 * hf + 1],
                                          b_pb[3 * hf + 2]] + b_xt96[lo:lo + n])

        fence_c = tuple(S.cut())
        parc = []
        for p in range(2):
            ar1.cur = xts_off if p == 0 else 60160
            sl = {}
            for name in ("QF", "GHS", "SIL", "OS", "OSQ"):
                sl[name] = ar1.take(2048)
            sl["WT"] = ar1.take(3 * 128)
            sm = {"WN": ar2.take(3 * 128), "WPD": ar2.take(3 * 128),
                  "rc": ar2.take(16), "oss": ar2.take(64)}
            bufs = {}
            for k in list(sl) + list(sm):
                bufs[k] = Buf(fence_c)
            parc.append((sl, sm, bufs))

        def p4c_pre(hm):
            half = hm // H
            h = hm % H
            p = hm % 2
            sl, sm, bb = parc[p]
            QF = sl["QF"]
            WT3 = sl["WT"].rearrange("p (a c) -> p a c", a=3)
            WN3 = sm["WN"].rearrange("p (a c) -> p a c", a=3)
            WPD3 = sm["WPD"].rearrange("p (a c) -> p a c", a=3)
            dsl = DWS3[:, hm * 3:(hm + 1) * 3, :]
            dbufs = [b_dw[hm * 3 + j] for j in range(3)]
            hh = half * 16 + h
            S.d("sync", lambda e, hh=hh: e.dma_start(
                QF, qT_h[hh], transpose=True),
                reads=[b_q_h[half]], writes=[bb["QF"]])
            S.d("sync", lambda e: e.dma_start(WN3, wnat3[:, :, h, :]),
                writes=[bb["WN"]])
            S.c("vector", lambda e: e.tensor_tensor(WPD3, WN3, dsl, ALU.add),
                reads=[bb["WN"]] + dbufs, writes=[bb["WPD"]])
            for j in range(3):
                S.c("vector", lambda e, j=j: e.scalar_tensor_tensor(
                    sl["OSQ"][:, 0:128], WPD3[:, j, :], 1.0, WPD3[:, j, :],
                    ALU.mult, ALU.mult, accum_out=sm["rc"][:, j:j + 1]),
                    reads=[bb["WPD"]], writes=[bb["rc"], bb["OSQ"]])
            S.c("scalar", lambda e: e.activation(
                sm["rc"][:, 4:7], sm["rc"][:, 0:3], AF.Sqrt),
                reads=[bb["rc"]], writes=[bb["rc"]])
            S.c("vector", lambda e: e.tensor_scalar(
                sm["rc"][:, 4:7], sm["rc"][:, 4:7], float(1e-5), None, ALU.add),
                reads=[], writes=[bb["rc"]])
            S.c("vector", lambda e: e.reciprocal(
                sm["rc"][:, 8:11], sm["rc"][:, 4:7]),
                reads=[], writes=[bb["rc"]])
            S.c("vector", lambda e, h=h: e.tensor_mul(
                sm["rc"][:, 8:11], sm["rc"][:, 8:11], wnorm_sb[:, h:48:16]),
                reads=[b_wnorm], writes=[bb["rc"]])
            S.c("vector", lambda e: e.tensor_mul(
                WT3, WPD3,
                sm["rc"][:, 8:11][:, :, None].broadcast_to([128, 3, 128])),
                reads=[bb["WPD"], bb["rc"]], writes=[bb["WT"]])
            for j in range(3):
                S.c("tensor", lambda e, j=j: e.transpose(
                    pbt[:, 0:128], WT3[:, j, :], ident[:, :]),
                    reads=[bb["WT"], b_ident], writes=[b_pbt])
                S.c("vector", lambda e, j=j: e.tensor_copy(
                    WT3[:, j, :], pbt[:, 0:128]),
                    reads=[], writes=[b_pbt, bb["WT"]])

        def p4c_apply(hm):
            half = hm // H
            h = hm % H
            p = hm % 2
            sl, sm, bb = parc[p]
            bank = [b_pb[3 * p + i] for i in range(2)]
            pbk = [pb[3 * p + i] for i in range(2)]
            u0 = half * 16
            QF = sl["QF"]
            GHS = sl["GHS"]
            SIL = sl["SIL"]
            OS = sl["OS"]
            OS3 = sl["OS"].rearrange("p (a c) -> p a c", a=16)
            OSQ3 = sl["OSQ"].rearrange("p (a c) -> p a c", a=16)
            WT3 = sl["WT"].rearrange("p (a c) -> p a c", a=3)
            for t in range(4):
                S.c("tensor", lambda e, t=t: e.matmul(
                    pbk[0][:, 0:512], WT3[:, 0, :], QF[:, t * 512:(t + 1) * 512],
                    start=True, stop=True),
                    reads=[bb["WT"], bb["QF"]], writes=[bank[0]])
                S.c("tensor", lambda e, t=t: e.matmul(
                    pbk[1][:, 0:512], WT3[:, 2, :], QF[:, t * 512:(t + 1) * 512],
                    start=True, stop=True),
                    reads=[bb["WT"], bb["QF"]], writes=[bank[1]])
                S.c("scalar", lambda e, t=t: e.activation(
                    SIL[:, t * 512:(t + 1) * 512], pbk[0][:, 0:512], AF.Silu),
                    reads=[], writes=[bank[0], bb["SIL"]])
                S.c("vector", lambda e, t=t: e.scalar_tensor_tensor(
                    GHS[:, t * 512:(t + 1) * 512], pbk[1][:, 0:512], 1.0,
                    SIL[:, t * 512:(t + 1) * 512], ALU.bypass, ALU.mult),
                    reads=[bb["SIL"]], writes=[bank[1], bb["GHS"]])
            for scg in range(4):
                bk = bank[scg % 2]
                pk = pbk[scg % 2]
                for si in range(4):
                    sc = scg * 4 + si
                    S.c("tensor", lambda e, sc=sc, si=si, pk=pk: e.matmul(
                        pk[:, si * 128:(si + 1) * 128],
                        GHS[:, sc * 128:(sc + 1) * 128], WT3[:, 1, :],
                        start=True, stop=True),
                        reads=[bb["GHS"], bb["WT"]], writes=[bk])
                S.c("vector", lambda e, scg=scg, pk=pk: e.tensor_copy(
                    OS[:, scg * 512:(scg + 1) * 512], pk[:, 0:512]),
                    reads=[], writes=[bk, bb["OS"]])
            oss3 = sm["oss"].rearrange("p (a c) -> p a c", a=4)
            S.c("vector", lambda e: e.tensor_mul(OSQ3, OS3, OS3),
                reads=[bb["OS"]], writes=[bb["OSQ"]])
            S.c("vector", lambda e: e.tensor_reduce(
                oss3[:, 0, :], OSQ3, AX.X, ALU.add),
                reads=[bb["OSQ"]], writes=[bb["oss"]])
            S.c("scalar", lambda e: e.activation(
                oss3[:, 1, :], oss3[:, 0, :], AF.Sqrt,
                scale=f32(1.0 / 128.0), bias=f32(1e-5)),
                reads=[], writes=[bb["oss"]])
            S.c("vector", lambda e: e.reciprocal(
                oss3[:, 2, :], oss3[:, 1, :]),
                reads=[], writes=[bb["oss"]])
            S.c("vector", lambda e: e.tensor_mul(
                OS3, OS3, oss3[:, 2, :][:, :, None].broadcast_to([128, 16, 128])),
                reads=[bb["oss"]], writes=[bb["OS"]])
            S.c("vector", lambda e: e.tensor_mul(
                OS3, OS3, gam[:, None, :].broadcast_to([128, 16, 128])),
                reads=[b_gam], writes=[bb["OS"]])
            S.d("sync", lambda e, half=half, h=h: e.dma_start(
                onrm4[:, half * 16 + h, :, :], OS3),
                reads=[bb["OS"]], writes=[b_onrm_h[half]])

        for hm in range(P4_HM):
            p4c_pre(hm)
            if hm > 0:
                p4c_apply(hm - 1)
            p6_feed(2)
        p4c_apply(P4_HM - 1)
        p6_feed(128)
        ar1.reset()
        ar2.reset()

    if STAGE >= 5:
        of_t = ar1.take(16 * 2048)
        of3 = of_t.rearrange("p (h t) -> p h t", h=16)
        b_of = ar1.buf()
        wo_buf = [ar1.take(16 * 128), ar1.take(16 * 128)]
        wo_b3 = [t.rearrange("p (a c) -> p a c", a=16) for t in wo_buf]
        b_wo = [ar1.buf(), ar1.buf()]
        xt_buf = [ar2.take(512) for _ in range(4)]
        b_xt = [ar2.buf() for _ in range(4)]
        ft_buf = [ar2.take(512) for _ in range(4)]
        b_ft = [ar2.buf() for _ in range(4)]
        po_buf = [ar2.take(512) for _ in range(4)]
        b_po = [ar2.buf() for _ in range(4)]
        for half in range(2):
            t0 = half * 2048
            for hh in range(16):
                S.d("sync", lambda e, hh=hh, half=half: e.dma_start(
                    of3[:, hh, :], onrm_h[half * 16 + hh], transpose=True),
                    reads=[b_onrm_h[half]], writes=[b_of])
            for oc in range(16):
                ws = oc % 2
                S.d("gpsimd", lambda e, oc=oc, ws=ws: e.dma_start(
                    wo_b3[ws], wo3[:, :, oc * 128:(oc + 1) * 128]),
                    writes=[b_wo[ws]])
                for t in range(4):
                    for hh in range(16):
                        S.c("tensor", lambda e, hh=hh, t=t, ws=ws: e.matmul(
                            pb[t][:, 0:512], wo_b3[ws][:, hh, :],
                            of3[:, hh, t * 512:(t + 1) * 512],
                            start=(hh == 0), stop=(hh == 15)),
                            reads=[b_wo[ws], b_of], writes=[b_pb[t]])
                    S.d("sync", lambda e, oc=oc, t=t, t0=t0: e.dma_start(
                        xt_buf[t], x_feat[oc * 128:(oc + 1) * 128,
                                          t0 + t * 512: t0 + (t + 1) * 512]),
                        writes=[b_xt[t]])
                    S.d("sync", lambda e, oc=oc, t=t, t0=t0: e.dma_start(
                        ft_buf[t], part_h[oc * 128:(oc + 1) * 128,
                                          t0 + t * 512: t0 + (t + 1) * 512]),
                        reads=[b_part], writes=[b_ft[t]])
                    S.c("vector", lambda e, t=t: e.scalar_tensor_tensor(
                        po_buf[t], pb[t][:, 0:512], 1.0, xt_buf[t],
                        ALU.bypass, ALU.add),
                        reads=[b_xt[t]], writes=[b_pb[t], b_po[t]])
                    S.c("vector", lambda e, t=t: e.tensor_add(
                        po_buf[t], po_buf[t], ft_buf[t]),
                        reads=[b_ft[t]], writes=[b_po[t]])
                    oid = S.d("sync", lambda e, oc=oc, t=t, t0=t0: e.dma_start(
                        out_feat[oc * 128:(oc + 1) * 128,
                                 t0 + t * 512: t0 + (t + 1) * 512], po_buf[t]),
                        reads=[b_po[t]], writes=[b_out])
                    out_dmas.append(oid)
        ar1.reset()
        ar2.reset()

    S.emit(sorted(set(out_dmas) | set(S.cut())))
    return nc


_NC_CACHE = None


def _get_nc():
    global _NC_CACHE
    if _NC_CACHE is None:
        _NC_CACHE = build_nc()
    return _NC_CACHE


def _prep_shared(ins):
    w0 = np.asarray(ins["L0_attn_w0"])
    w1 = np.asarray(ins["L0_attn_w1"])
    w2 = np.asarray(ins["L0_attn_w2"])
    return {
        "wqkv_t": np.ascontiguousarray(
            np.concatenate([np.asarray(ins["L0_attn_to_qkv_w"]),
                            np.asarray(ins["L0_attn_lr_proj_w"])], 0).T),
        "wg_t": np.ascontiguousarray(np.asarray(ins["L0_ffn_gate_w"]).T),
        "wu_t": np.ascontiguousarray(np.asarray(ins["L0_ffn_up_w"]).T),
        "wd_t": np.ascontiguousarray(np.asarray(ins["L0_ffn_down_w"]).T),
        "wo_t": np.ascontiguousarray(np.asarray(ins["L0_attn_o_proj_w"]).T),
        "w_nat": np.ascontiguousarray(
            np.stack([w0, w1, w2], 0).transpose(2, 0, 1, 3)
            .reshape(128, 3 * H * 128)),
        "w02t": np.ascontiguousarray(
            np.concatenate([w0.transpose(0, 2, 1), w2.transpose(0, 2, 1)], 2)
            .transpose(1, 0, 2).reshape(128, H * 256)),
        "gate_s_r": np.ascontiguousarray(
            np.asarray(ins["L0_ffn_gate_s"]).reshape(NI, 128).T),
        "up_s_r": np.ascontiguousarray(
            np.asarray(ins["L0_ffn_up_s"]).reshape(NI, 128).T),
        "down_s_r": np.ascontiguousarray(
            np.asarray(ins["L0_ffn_down_s"]).reshape(16, 128).T),
        "onorm_r": np.ascontiguousarray(
            np.asarray(ins["L0_attn_o_norm_w"]).reshape(1, 128)),
    }


def kernel(**inputs):
    x = np.asarray(inputs["L0_attn_x"], dtype=np.float32)
    shared = _prep_shared(inputs)
    in_maps = []
    for c in range(NCORES):
        xc = x[2 * c:2 * c + 2]
        m = dict(shared)
        m["x_tok"] = np.ascontiguousarray(xc.reshape(T, D))
        m["x_feat"] = np.ascontiguousarray(
            np.concatenate([xc[0].T, xc[1].T], axis=1))
        in_maps.append(m)
    nc = _get_nc()
    res = run_bass_kernel_spmd(nc, in_maps, core_ids=list(range(NCORES)))
    out = np.empty((16, SEQ, D), dtype=np.float32)
    for c in range(NCORES):
        of = res.results[c]["out_feat"]
        out[2 * c] = of[:, 0:SEQ].T
        out[2 * c + 1] = of[:, SEQ:2 * SEQ].T
    return out


# version 13 (final)
# speedup vs baseline: 1.0505x; 1.0313x over previous
import numpy as np
import ml_dtypes
import concourse.bass as bass
import concourse.mybir as mybir
from concourse.bass_utils import run_bass_kernel_spmd

F32 = mybir.dt.float32
BF16 = mybir.dt.bfloat16
AF = mybir.ActivationFunctionType
ALU = mybir.AluOpType
AX = mybir.AxisListType
f32 = np.float32

NCORES = 8
D = 2048
SEQ = 2048
T = 4096
H = 16
NQKV = 6144
NLR = 48
INTER = 5632
NI = INTER // 128
NTC = T // 128
NS_COEFS = [(4.0848, -6.8946, 2.927), (3.9505, -6.3029, 2.6377),
            (3.7418, -5.5913, 2.3037), (2.8769, -3.1427, 1.2046),
            (2.8366, -3.0525, 1.2012)]
BASE_LR_INV = 0.01 + float(np.log(-np.expm1(-0.01)))

STAGE = 6
DEBUG_TAPS = False
P4_HM = 2 * H
P4_SECT = 9

ENGS = ("sync", "gpsimd", "tensor", "vector", "scalar")


NLANES = 10


class Buf:
    __slots__ = ("ws", "r", "excl")
    def __init__(self, fence=(), excl=False):
        self.ws = []
        self.r = list(fence)
        self.excl = excl


class Sched:
    def __init__(self, nc):
        self.nc = nc
        self.ops = []
        self.dma_cnt = {}

    def cut(self):
        last = {}
        for op in self.ops:
            if op["kind"] == "d":
                last[(op["eng"], "d", op["lane"])] = op["i"]
            else:
                last[(op["eng"], "c")] = op["i"]
        return sorted(last.values())

    def add(self, eng, kind, fn, reads=(), writes=(), deps=()):
        i = len(self.ops)
        dep = set()
        waw = set()
        for d in deps:
            if d is not None:
                assert d < i
                dep.add(d)
        writes = list(writes) + [b for b in reads if b.excl]
        reads = [b for b in reads if not b.excl]
        for b in reads:
            dep.update(b.ws)
        for b in writes:
            waw.update(b.ws)
            dep.update(b.r)
        dep -= waw
        op = dict(i=i, eng=eng, kind=kind, fn=fn,
                  deps=sorted(dep), waw=sorted(waw),
                  inc=False, idx=None, lane=None, lidx=None)
        if kind == "d":
            c = self.dma_cnt.get(eng, 0)
            self.dma_cnt[eng] = c + 1
            op["lane"] = c % NLANES
            op["lidx"] = c // NLANES + 1
        self.ops.append(op)
        for b in reads:
            b.r.append(i)
        for b in writes:
            b.ws.append(i)
        return i

    def c(self, eng, fn, reads=(), writes=(), deps=()):
        return self.add(eng, "c", fn, reads, writes, deps)

    def d(self, q, fn, reads=(), writes=(), deps=()):
        return self.add(q, "d", fn, reads, writes, deps)

    @staticmethod
    def _skip(p, op, is_waw):
        if p["eng"] != op["eng"]:
            return False
        if p["kind"] == "c" and op["kind"] == "c":
            return p["eng"] == "tensor"
        if is_waw and p["kind"] == "d" and op["kind"] == "d":
            return True
        return False

    @staticmethod
    def _key(p):
        if p["kind"] == "d":
            return (p["eng"], "d", p["lane"])
        return (p["eng"], "c")

    def emit(self, final_deps):
        nc = self.nc
        self.c("sync", lambda e: e.nop(), deps=final_deps)
        for op in self.ops:
            if op["kind"] == "d":
                op["inc"] = True
            for d in op["deps"]:
                p = self.ops[d]
                if not self._skip(p, op, False):
                    p["inc"] = True
            for d in op["waw"]:
                p = self.ops[d]
                if not self._skip(p, op, True):
                    p["inc"] = True
        counters = {}
        for op in self.ops:
            if op["kind"] == "d":
                op["idx"] = op["lidx"]
            elif op["inc"]:
                key = self._key(op)
                counters[key] = counters.get(key, 0) + 1
                op["idx"] = counters[key]
        keys = set(counters)
        for op in self.ops:
            if op["kind"] == "d":
                keys.add(self._key(op))
        sem_of = {key: nc.alloc_semaphore("s_" + "_".join(map(str, key)))
                  for key in sorted(keys)}
        for op in self.ops:
            waits = {}
            for dlist, is_waw in ((op["deps"], False), (op["waw"], True)):
                for d in dlist:
                    p = self.ops[d]
                    if self._skip(p, op, is_waw):
                        continue
                    key = self._key(p)
                    amt = 16 if p["kind"] == "d" else 1
                    v = p["idx"] * amt
                    if waits.get(key, 0) < v:
                        waits[key] = v
            if op["kind"] == "d" and op["lidx"] > 1:
                key = self._key(op)
                v = (op["lidx"] - 1) * 16
                if waits.get(key, 0) < v:
                    waits[key] = v
            op["waits"] = waits
        per_eng = {e: [] for e in ENGS}
        for op in self.ops:
            per_eng[op["eng"]].append(op)

        with nc.Block() as blk:
            for eng_name in ENGS:
                ops_list = per_eng[eng_name]
                if not ops_list:
                    continue

                def body(e, ops_list=ops_list):
                    hw = {}
                    for op in ops_list:
                        for key in sorted(op["waits"]):
                            v = op["waits"][key]
                            if hw.get(key, 0) >= v:
                                continue
                            e.wait_ge(sem_of[key], v)
                            hw[key] = v
                        inst = op["fn"](e)
                        if op["kind"] == "d":
                            inst.then_inc(sem_of[self._key(op)], 16)
                        elif op["inc"]:
                            inst.then_inc(sem_of[self._key(op)], 1)

                getattr(blk, eng_name)(body)


class Arena:
    def __init__(self, sched, tensor, elems):
        self.S = sched
        self.t = tensor
        self.elems = elems
        self.cur = 0
        self.fence = ()

    def take(self, n):
        assert self.cur + n <= self.elems, (self.cur, n, self.elems)
        off = self.cur
        self.cur += n
        return self.t[:, off:off + n]

    def buf(self):
        return Buf(self.fence)

    def reset(self):
        self.cur = 0
        self.fence = tuple(self.S.cut())


def build_nc():
    nc = bass.Bass("TRN2", target_bir_lowering=False)

    def I(name, shape):
        return nc.dram_tensor(name, list(shape), F32, kind="ExternalInput").ap()

    x_tok = I("x_tok", (T, D))
    x_feat = I("x_feat", (D, T))
    wqkv_t = I("wqkv_t", (D, NQKV + NLR))
    wg_t = I("wg_t", (D, INTER))
    wu_t = I("wu_t", (D, INTER))
    wd_t = I("wd_t", (INTER, D))
    wo_t = I("wo_t", (D, D))
    w_nat = I("w_nat", (128, 3 * H * 128))
    w02t = I("w02t", (128, H * 256))
    gate_s_r = I("gate_s_r", (128, NI))
    up_s_r = I("up_s_r", (128, NI))
    down_s_r = I("down_s_r", (128, 16))
    onorm_r = I("onorm_r", (1, 128))

    out_feat = nc.dram_tensor("out_feat", [D, T], F32, kind="ExternalOutput").ap()

    sk = "ExternalOutput" if DEBUG_TAPS else "Internal"
    xn_h = nc.dram_tensor("xn_h", [16, T, 128], BF16, kind=sk).ap()
    qT_h = nc.dram_tensor("qT_h", [2 * H, SEQ, 128], BF16, kind=sk).ap()
    kT_h = nc.dram_tensor("kT_h", [2 * H, SEQ, 128], BF16, kind=sk).ap()
    vT_h = nc.dram_tensor("vT_h", [2 * H, SEQ, 128], BF16, kind=sk).ap()
    hdn_h = nc.dram_tensor("hdn_h", [INTER, T], BF16, kind=sk).ap()
    wd_bf_h = nc.dram_tensor("wd_bf_h", [INTER, D], BF16, kind="Internal").ap()
    onrm_h = nc.dram_tensor("onrm_h", [2 * H, SEQ, 128], BF16, kind=sk).ap()
    part_h = nc.dram_tensor("part_h", [D, T], F32, kind=sk).ap()

    qT4 = qT_h.rearrange("g (u p) c -> p g u c", p=128)
    kT4 = kT_h.rearrange("g (u p) c -> p g u c", p=128)
    vT4 = vT_h.rearrange("g (u p) c -> p g u c", p=128)
    onrm4 = onrm_h.rearrange("g (u p) c -> p g u c", p=128)
    xn_tok3 = xn_h.rearrange("a (u p) c -> p u a c", p=128)
    hdn3 = hdn_h.rearrange("(a p) t -> p a t", p=128)
    wd3 = wd_t.rearrange("(a p) c -> p a c", p=128)
    wdbf3 = wd_bf_h.rearrange("(a p) c -> p a c", p=128)
    wqkv3 = wqkv_t.rearrange("(a p) o -> p a o", p=128)
    wg3 = wg_t.rearrange("(a p) o -> p a o", p=128)
    wu3 = wu_t.rearrange("(a p) o -> p a o", p=128)
    wo3 = wo_t.rearrange("(a p) o -> p a o", p=128)
    wnat3 = w_nat.rearrange("p (j h c) -> p j h c", j=3, h=H)
    w02t3 = w02t.rearrange("p (h c) -> p h c", h=H)

    A = nc.alloc_sbuf_tensor
    ident = A("ident", [128, 128], BF16)
    gam = A("gam", [128, 128], BF16)
    aI = A("aI", [128, 5 * 128], BF16)
    onesrow = A("onesrow", [1, 128], BF16)
    onescol = A("onescol", [128, 1], BF16)
    onorm_bf = A("onorm_bf", [1, 128], BF16)
    gate_s_sb = A("gate_s_sb", [128, NI], F32)
    up_s_sb = A("up_s_sb", [128, NI], F32)
    down_s_sb = A("down_s_sb", [128, 16], F32)
    lr_sb = A("lr_sb", [128, NTC * NLR], F32)
    lr3 = lr_sb[:, :].rearrange("p (u c) -> p u c", c=NLR)
    ssq_q = A("ssq_q", [128, NTC * H], F32)
    ssq_k = A("ssq_k", [128, NTC * H], F32)
    rq_sb = A("rq_sb", [128, NTC * H], F32)
    rk_sb = A("rk_sb", [128, NTC * H], F32)
    ssq_q3 = ssq_q[:, :].rearrange("p (u h) -> p u h", h=H)
    ssq_k3 = ssq_k[:, :].rearrange("p (u h) -> p u h", h=H)
    rq3 = rq_sb[:, :].rearrange("p (u h) -> p u h", h=H)
    rk3 = rk_sb[:, :].rearrange("p (u h) -> p u h", h=H)
    w02t_sb = A("w02t_sb", [128, H * 256], BF16)
    w02t_sb3 = w02t_sb[:, :].rearrange("p (h c) -> p h c", h=H)
    w1n_sb = A("w1n_sb", [128, H * 128], BF16)
    w1n_sb3 = w1n_sb[:, :].rearrange("p (h c) -> p h c", h=H)
    wnorm_sb = A("wnorm_sb", [128, 3 * H], F32)

    AR1_E = 79232
    AR2_E = 6 * 1024
    ar1_t = A("ar1", [128, AR1_E], BF16)
    ar2_t = A("ar2", [128, AR2_E], F32)

    pbig = [nc.alloc_psum_tensor(f"pbig{i}", [128, 1536], F32) for i in range(2)]
    pb = [pbig[i // 3][:, (i % 3) * 512:(i % 3) * 512 + 512] for i in range(6)]
    pb.append(nc.alloc_psum_tensor("pb6", [128, 512], F32))
    pbt = nc.alloc_psum_tensor("pbt", [128, 1024], BF16)

    S = Sched(nc)
    ar1 = Arena(S, ar1_t, AR1_E)
    ar2 = Arena(S, ar2_t, AR2_E)

    B = Buf
    b_ident, b_gam, b_aI = B(), B(), B()
    b_onesrow, b_onescol, b_onorm = B(), B(), B()
    b_scales = B()
    b_lr, b_ssq_q, b_ssq_k, b_rq, b_rk = B(), B(), B(), B(), B()
    b_w02t, b_w1n, b_wnorm = B(), B(), B()
    b_pb = [B(excl=True) for _ in range(7)]
    b_pbt = B(excl=True)
    b_xn_h = [B(), B()]
    b_q_h = [B(), B()]
    b_kv_h = [B(), B()]
    b_hdn_h = [B(), B()]
    b_onrm_h = [B(), B()]
    b_part = B()
    b_wdbf = B()
    b_out = B()
    out_dmas = []

    S.c("gpsimd", lambda e: e.memset(ident[:, :], 0.0), writes=[b_ident])
    S.c("gpsimd", lambda e: e.affine_select(
        out=ident[:, :], in_=ident[:, :], compare_op=ALU.not_equal,
        fill=1.0, base=0, pattern=[[-1, 128]], channel_multiplier=1),
        writes=[b_ident])
    S.c("gpsimd", lambda e: e.memset(aI[:, :], 0.0), writes=[b_aI])
    for k, (a_k, _, _) in enumerate(NS_COEFS):
        S.c("gpsimd", lambda e, k=k, a_k=a_k: e.affine_select(
            out=aI[:, k * 128:(k + 1) * 128], in_=aI[:, k * 128:(k + 1) * 128],
            compare_op=ALU.not_equal, fill=float(a_k), base=0,
            pattern=[[-1, 128]], channel_multiplier=1), writes=[b_aI])
    S.c("vector", lambda e: e.memset(onesrow[:, :], 1.0), writes=[b_onesrow])
    S.c("vector", lambda e: e.memset(onescol[:, :], 1.0), writes=[b_onescol])
    S.d("sync", lambda e: e.dma_start(gate_s_sb[:, :], gate_s_r[:, :]), writes=[b_scales])
    S.d("sync", lambda e: e.dma_start(up_s_sb[:, :], up_s_r[:, :]), writes=[b_scales])
    S.d("sync", lambda e: e.dma_start(down_s_sb[:, :], down_s_r[:, :]), writes=[b_scales])
    S.d("gpsimd", lambda e: e.dma_start(onorm_bf[:, :], onorm_r[:, :]), writes=[b_onorm])
    S.d("gpsimd", lambda e: e.dma_start(w02t_sb[:, :], w02t[:, :]), writes=[b_w02t])
    S.d("gpsimd", lambda e: e.dma_start(w1n_sb[:, :], w_nat[:, 2048:4096]), writes=[b_w1n])
    S.c("tensor", lambda e: e.matmul(pb[6][:, 0:128], onesrow[:, :], onorm_bf[:, :],
                                     start=True, stop=True),
        reads=[b_onesrow, b_onorm], writes=[b_pb[6]])
    S.c("vector", lambda e: e.tensor_copy(gam[:, :], pb[6][:, 0:128]),
        reads=[b_pb[6]], writes=[b_gam])
    wtmp = ar2.take(6144)
    wtmp3 = wtmp.rearrange("p (j h c) -> p j h c", j=3, h=H)
    b_wtmp = ar2.buf()
    junk = ar1.take(128)
    b_junk = ar1.buf()
    S.d("sync", lambda e: e.dma_start(wtmp, w_nat[:, :]), writes=[b_wtmp])
    for j in range(3):
        for h in range(H):
            S.c("vector", lambda e, j=j, h=h: e.scalar_tensor_tensor(
                junk, wtmp3[:, j, h, :], 1.0, wtmp3[:, j, h, :],
                ALU.mult, ALU.mult,
                accum_out=wnorm_sb[:, j * H + h: j * H + h + 1]),
                reads=[b_wtmp], writes=[b_wnorm, b_junk])
    S.c("scalar", lambda e: e.activation(wnorm_sb[:, :], wnorm_sb[:, :], AF.Sqrt),
        reads=[b_wnorm], writes=[b_wnorm])
    ar1.reset()
    ar2.reset()

    if STAGE >= 1:
        xw = [ar2.take(2048) for _ in range(2)] + \
            [ar1.take(4096).bitcast(F32) for _ in range(2)]
        b_xw = [ar2.buf(), ar2.buf(), ar1.buf(), ar1.buf()]
        st_t = ar2.take(NTC * 8)
        st3 = st_t.rearrange("p (u c) -> p u c", c=8)
        b_st = ar2.buf()
        sq_scr = ar1.take(2048)
        b_sq = ar1.buf()
        xo = [ar1.take(2048), ar1.take(2048)]
        b_xo = [ar1.buf(), ar1.buf()]
        for i in range(NTC):
            w = i % 4
            half = i // 16
            S.d("sync" if i % 2 == 0 else "gpsimd",
                lambda e, i=i, w=w: e.dma_start(
                    xw[w], x_tok[i * 128:(i + 1) * 128, :]), writes=[b_xw[w]])
            S.c("vector", lambda e, i=i, w=w: e.reduce_sum(
                st3[:, i, 0:1], xw[w], AX.X), reads=[b_xw[w]], writes=[b_st])
            S.c("scalar", lambda e, i=i, w=w: e.activation(
                sq_scr, xw[w], AF.Square, accum_out=st3[:, i, 1:2]),
                reads=[b_xw[w]], writes=[b_st, b_sq])
            S.c("vector", lambda e, i=i: e.tensor_scalar(
                st3[:, i, 2:3], st3[:, i, 0:1], float(-1.0 / D), None, ALU.mult),
                reads=[b_st], writes=[b_st])
            S.c("vector", lambda e, i=i: e.scalar_tensor_tensor(
                st3[:, i, 3:4], st3[:, i, 0:1], st3[:, i, 2:3], st3[:, i, 1:2],
                ALU.mult, ALU.add), reads=[b_st], writes=[b_st])
            S.c("scalar", lambda e, i=i: e.activation(
                st3[:, i, 4:5], st3[:, i, 3:4], AF.Sqrt,
                scale=f32(1.0 / D), bias=f32(1e-5)), reads=[b_st], writes=[b_st])
            S.c("vector", lambda e, i=i: e.reciprocal(
                st3[:, i, 5:6], st3[:, i, 4:5]), reads=[b_st], writes=[b_st])
            S.c("vector", lambda e, i=i: e.tensor_mul(
                st3[:, i, 6:7], st3[:, i, 2:3], st3[:, i, 5:6]),
                reads=[b_st], writes=[b_st])
            S.c("scalar", lambda e, i=i, w=w: e.activation(
                xo[w % 2], xw[w], AF.Identity,
                bias=st3[:, i, 6:7], scale=st3[:, i, 5:6]),
                reads=[b_xw[w], b_st], writes=[b_xo[w % 2]])
            S.d("sync", lambda e, i=i, w=w: e.dma_start(
                xn_tok3[:, i, :, :],
                xo[w % 2].rearrange("p (a c) -> p a c", a=16)),
                reads=[b_xo[w % 2]], writes=[b_xn_h[half]])
        ar1.reset()
        ar2.reset()

    QOFB = 4
    NOFB = 12
    if STAGE >= 2:
        xn_t = ar1.take(16 * 2048)
        xn3 = xn_t.rearrange("p (a t) -> p a t", a=16)
        b_xn = ar1.buf()
        wq_buf = [ar1.take(16 * 512), ar1.take(16 * 512)]
        wq3 = [t.rearrange("p (a o) -> p a o", a=16) for t in wq_buf]
        b_wq = [ar1.buf(), ar1.buf()]
        ev_st = [ar1.take(512) for _ in range(3)]
        b_ev = [ar1.buf() for _ in range(3)]
        sqt = [ar1.take(512) for _ in range(2)]
        b_sqt = [ar1.buf() for _ in range(2)]
        lr_tmp = [ar2.take(NLR) for _ in range(3)]
        b_lrt = [ar2.buf() for _ in range(3)]
        r_tmp = ar2.take(256)
        b_rtmp = ar2.buf()
        wgu_buf = [ar1.take(16 * 128) for _ in range(4)]
        wgu3 = [t.rearrange("p (a o) -> p a o", a=16) for t in wgu_buf]
        b_wgu = [ar1.buf() for _ in range(4)]
        gtmp = [ar1.take(512) for _ in range(4)]
        b_gt = [ar1.buf() for _ in range(4)]
        htmp = [ar1.take(512) for _ in range(4)]
        b_ht = [ar1.buf() for _ in range(4)]
        wdc = [ar1.take(2048), ar1.take(2048)]
        wdc3 = [t.rearrange("p (a c) -> p a c", a=1) for t in wdc]
        b_wdc = [ar1.buf(), ar1.buf()]
        qp = [ar1.take(2048), ar1.take(2048)]
        qp3 = [t.rearrange("p (h c) -> p h c", h=16) for t in qp]
        b_qp = [ar1.buf(), ar1.buf()]

    def p3_half(half):
        for a in range(16):
            S.d("sync", lambda e, a=a, half=half: e.dma_start(
                xn3[:, a, :],
                xn_h[a, half * 2048:(half + 1) * 2048, :],
                transpose=True),
                reads=[b_xn_h[half]], writes=[b_xn])
        for ofb in range(NOFB + 1):
            wslot = ofb % 2
            is_lr = (ofb == NOFB)
            width = NLR if is_lr else 512
            S.d("gpsimd", lambda e, ofb=ofb, wslot=wslot, width=width: e.dma_start(
                wq3[wslot][:, :, 0:width],
                wqkv3[:, :, ofb * 512: ofb * 512 + width]),
                writes=[b_wq[wslot]])
            for tc in range(16):
                u = half * 16 + tc
                bank = tc % 3
                for a in range(16):
                    S.c("tensor", lambda e, a=a, tc=tc, wslot=wslot, width=width, bank=bank:
                        e.matmul(pb[bank][:, 0:width],
                                 xn3[:, a, tc * 128:(tc + 1) * 128],
                                 wq3[wslot][:, a, 0:width],
                                 start=(a == 0), stop=(a == 15)),
                        reads=[b_xn, b_wq[wslot]], writes=[b_pb[bank]])
                if is_lr:
                    lt = tc % 3
                    S.c("scalar", lambda e, bank=bank, lt=lt: e.activation(
                        lr_tmp[lt], pb[bank][:, 0:NLR], AF.Abs,
                        bias=f32(BASE_LR_INV)),
                        reads=[b_pb[bank]], writes=[b_lrt[lt]])
                    S.c("scalar", lambda e, lt=lt: e.activation(
                        lr_tmp[lt], lr_tmp[lt], AF.Exp, scale=f32(-1.0)),
                        reads=[b_lrt[lt]], writes=[b_lrt[lt]])
                    S.c("scalar", lambda e, lt=lt: e.activation(
                        lr_tmp[lt], lr_tmp[lt], AF.Ln, bias=f32(1.0)),
                        reads=[b_lrt[lt]], writes=[b_lrt[lt]])
                    S.c("scalar", lambda e, bank=bank, u=u: e.activation(
                        lr3[:, u, :], pb[bank][:, 0:NLR], AF.Relu,
                        bias=f32(BASE_LR_INV)),
                        reads=[b_pb[bank]], writes=[b_lr])
                    S.c("vector", lambda e, u=u, lt=lt: e.tensor_add(
                        lr3[:, u, :], lr3[:, u, :], lr_tmp[lt]),
                        reads=[b_lrt[lt]], writes=[b_lr])
                else:
                    es = tc % 3
                    S.c("scalar", lambda e, bank=bank, es=es: e.activation(
                        ev_st[es], pb[bank][:, 0:512], AF.Silu),
                        reads=[b_pb[bank]], writes=[b_ev[es]])
                    hbuf = b_q_h[half] if ofb < QOFB else b_kv_h[half]
                    dstT = (qT4 if ofb < QOFB else
                            (kT4 if ofb < 2 * QOFB else vT4))
                    g0 = half * 16 + (ofb % QOFB) * 4
                    S.d("sync", lambda e, u=u, tc=tc, g0=g0, es=es, dstT=dstT:
                        e.dma_start(
                            dstT[:, g0:g0 + 4, tc, :],
                            ev_st[es].rearrange("p (g c) -> p g c", g=4)),
                        reads=[b_ev[es]], writes=[hbuf])
                    if ofb < 8:
                        ssq3 = ssq_q3 if ofb < QOFB else ssq_k3
                        bs = b_ssq_q if ofb < QOFB else b_ssq_k
                        h0 = (ofb % 4) * 4
                        qs = tc % 2
                        ev3 = ev_st[es].rearrange("p (a b) -> p a b", b=128)
                        sq3 = sqt[qs].rearrange("p (a b) -> p a b", b=128)
                        S.c("vector", lambda e, es=es, qs=qs, ev3=ev3, sq3=sq3:
                            e.tensor_mul(sq3, ev3, ev3),
                            reads=[b_ev[es]], writes=[b_sqt[qs]])
                        S.c("vector", lambda e, qs=qs, sq3=sq3, u=u, h0=h0, ssq3=ssq3:
                            e.tensor_reduce(ssq3[:, u, h0:h0 + 4], sq3, AX.X, ALU.add),
                            reads=[b_sqt[qs]], writes=[bs])
        for (ssq3v, bs, r3v, br) in ((ssq_q3, b_ssq_q, rq3, b_rq),
                                     (ssq_k3, b_ssq_k, rk3, b_rk)):
            sl = slice(half * 16, (half + 1) * 16)
            S.c("scalar", lambda e, ssq3v=ssq3v, sl=sl, bs=bs: e.activation(
                r_tmp, ssq3v[:, sl, :].rearrange("p a b -> p (a b)"), AF.Ln,
                bias=f32(1e-30)), reads=[bs], writes=[b_rtmp])
            S.c("scalar", lambda e: e.activation(
                r_tmp, r_tmp, AF.Exp, scale=f32(0.5)),
                reads=[b_rtmp], writes=[b_rtmp])
            S.c("vector", lambda e: e.tensor_scalar(
                r_tmp, r_tmp, float(1e-5), None, ALU.add),
                reads=[b_rtmp], writes=[b_rtmp])
            S.c("vector", lambda e, r3v=r3v, sl=sl, br=br: e.reciprocal(
                r3v[:, sl, :].rearrange("p a b -> p (a b)"), r_tmp),
                reads=[b_rtmp], writes=[br])
        for tc in range(16):
            u = half * 16 + tc
            w = tc % 2
            g0 = half * 16
            S.d("sync", lambda e, tc=tc, w=w, g0=g0: e.dma_start(
                qp3[w], qT4[:, g0:g0 + 16, tc, :]),
                reads=[b_q_h[half]], writes=[b_qp[w]])
            S.c("vector", lambda e, u=u, w=w: e.tensor_mul(
                qp3[w], qp3[w],
                rq3[:, u, :][:, :, None].broadcast_to([128, 16, 128])),
                reads=[b_rq, b_qp[w]], writes=[b_qp[w]])
            S.d("sync", lambda e, tc=tc, w=w, g0=g0: e.dma_start(
                qT4[:, g0:g0 + 16, tc, :], qp3[w]),
                reads=[b_qp[w]], writes=[b_q_h[half]])

    def p3b_half(half):
        bank_map = [[0, 1, 2, 3], [4, 5, 6, 3]]
        for of in range(NI):
            sg = of % 2
            S.d("gpsimd", lambda e, of=of, sg=sg: e.dma_start(
                wgu3[2 * sg], wg3[:, :, of * 128:(of + 1) * 128]),
                writes=[b_wgu[2 * sg]])
            S.d("gpsimd", lambda e, of=of, sg=sg: e.dma_start(
                wgu3[2 * sg + 1], wu3[:, :, of * 128:(of + 1) * 128]),
                writes=[b_wgu[2 * sg + 1]])
            for tq in range(2):
                bm = bank_map[tq]
                for gi in range(2):
                    wb = b_wgu[2 * sg + gi]
                    wv = wgu3[2 * sg + gi]
                    for a in range(16):
                        for t in range(2):
                            bank = bm[gi * 2 + t]
                            S.c("tensor", lambda e, a=a, t=t, tq=tq, wv=wv, bank=bank:
                                e.matmul(pb[bank][:, 0:512], wv[:, a, :],
                                         xn3[:, a, tq * 1024 + t * 512:
                                             tq * 1024 + (t + 1) * 512],
                                         start=(a == 0), stop=(a == 15)),
                                reads=[b_xn, wb], writes=[b_pb[bank]])
                for t in range(2):
                    es = 2 * tq + t
                    gb, ub = bm[t], bm[2 + t]
                    S.c("scalar", lambda e, of=of, es=es, gb=gb: e.activation(
                        gtmp[es], pb[gb][:, 0:512], AF.Silu,
                        scale=gate_s_sb[:, of:of + 1]),
                        reads=[b_pb[gb], b_scales], writes=[b_gt[es]])
                    S.c("vector", lambda e, of=of, es=es, ub=ub: e.scalar_tensor_tensor(
                        htmp[es], pb[ub][:, 0:512], up_s_sb[:, of:of + 1],
                        gtmp[es], ALU.mult, ALU.mult),
                        reads=[b_pb[ub], b_gt[es], b_scales], writes=[b_ht[es]])
                    off = half * 2048 + tq * 1024 + t * 512
                    S.d("sync", lambda e, of=of, off=off, es=es: e.dma_start(
                        hdn3[:, of, off:off + 512], htmp[es]),
                        reads=[b_ht[es]], writes=[b_hdn_h[half]])
        if half == 0:
            for cc in range(NI):
                w = cc % 2
                S.d("gpsimd", lambda e, cc=cc, w=w: e.dma_start(
                    wdc3[w], wd3[:, cc:cc + 1, :]), writes=[b_wdc[w]])
                S.d("sync", lambda e, cc=cc, w=w: e.dma_start(
                    wdbf3[:, cc:cc + 1, :], wdc3[w]),
                    reads=[b_wdc[w]], writes=[b_wdbf])

    if STAGE >= 2:
        p3_half(0)
        if STAGE >= 3:
            p3b_half(0)
        p3_half(1)
        if STAGE >= 3:
            p3b_half(1)
        ar1.reset()
        ar2.reset()

    if STAGE >= 4:
        NMAT = 3 * P4_HM
        DWS_t = ar1.take(96 * 128)
        DWS3 = DWS_t.rearrange("p (n c) -> p n c", n=96)
        b_dw = [ar1.buf() for _ in range(96)]
        xts_off = ar1.cur
        XTS_t = ar1.take(96 * 128)
        XTS3 = XTS_t.rearrange("p (n c) -> p n c", n=96)
        b_xt96 = [ar1.buf() for _ in range(96)]
        csb96 = ar1.take(128)
        b_csb = ar1.buf()
        cs96 = ar2.take(96)
        b_cs = ar2.buf()
        scl96 = ar2.take(96)
        b_scl = ar2.buf()
        ar1_mark = ar1.cur
        ar2_mark = ar2.cur

        par = []
        for p in range(2):
            sl = {}
            for name in ("KT", "VT", "SG", "DS", "HID", "DGA", "DB2"):
                sl[name] = ar1.take(2048)
            sl["GH"] = ar1.take(16 * 384)
            sm = {"lrk": ar2.take(32)}
            bufs = {k: ar1.buf() for k in list(sl) + ["lrk"]}
            bufs["DH"] = bufs["GH"]
            par.append((sl, sm, bufs))
        kfr = [ar1.take(2048) for _ in range(3)]
        vfr = [ar1.take(2048) for _ in range(3)]
        b_kfr = [ar1.buf() for _ in range(3)]
        b_vfr = [ar1.buf() for _ in range(3)]

        def p4a_streams(hm):
            half = hm // H
            h = hm % H
            hh = half * 16 + h
            r = hm % 3
            S.d("sync", lambda e, hh=hh, r=r: e.dma_start(
                kfr[r], kT_h[hh], transpose=True),
                reads=[b_kv_h[half]], writes=[b_kfr[r]])
            S.d("sync", lambda e, hh=hh, r=r: e.dma_start(
                vfr[r], vT_h[hh], transpose=True),
                reads=[b_kv_h[half]], writes=[b_vfr[r]])

        def p4a(hm, dwg=None):
            half = hm // H
            h = hm % H
            p = hm % 2
            r = hm % 3
            sl, sm, bb = par[p]
            bb = dict(bb)
            bb["KF"] = b_kfr[r]
            bb["VF"] = b_vfr[r]
            bank = [b_pb[3 * p + i] for i in range(3)]
            pbk = [pb[3 * p + i] for i in range(3)]
            u0 = half * 16
            KF, VF, KT, VT = kfr[r], vfr[r], sl["KT"], sl["VT"]
            GH3 = sl["GH"].rearrange("p (a c) -> p a c", a=16)
            gba = GH3[:, :, 0:128]
            hbm = GH3[:, :, 128:256]
            gbahbm = GH3[:, :, 0:256]
            DH3 = GH3[:, :, 256:384]
            KF3 = KF.rearrange("p (a c) -> p a c", a=16)
            VF3 = VF.rearrange("p (a c) -> p a c", a=16)
            KT3 = KT.rearrange("p (a c) -> p a c", a=16)
            VT3 = VT.rearrange("p (a c) -> p a c", a=16)
            SG3 = sl["SG"].rearrange("p (a c) -> p a c", a=16)
            DS3 = sl["DS"].rearrange("p (a c) -> p a c", a=16)
            HID3 = sl["HID"].rearrange("p (a c) -> p a c", a=16)
            DGA3 = sl["DGA"].rearrange("p (a c) -> p a c", a=16)
            DB23 = sl["DB2"].rearrange("p (a c) -> p a c", a=16)

            hh = half * 16 + h
            S.d("sync", lambda e, hh=hh: e.dma_start(
                KT3, kT4[:, hh, :, :]),
                reads=[b_kv_h[half]], writes=[bb["KT"]])
            S.d("sync", lambda e, hh=hh: e.dma_start(
                VT3, vT4[:, hh, :, :]),
                reads=[b_kv_h[half]], writes=[bb["VT"]])

            pgrp = pbig[p]
            for w0sc in range(0, 16, 3):
                ns = min(3, 16 - w0sc)
                for i in range(ns):
                    sc = w0sc + i
                    pk = pbk[i]
                    S.c("tensor", lambda e, sc=sc, pk=pk: e.matmul(
                        pk[:, 0:256], KF3[:, sc, :], w02t_sb3[:, h, :],
                        start=True, stop=True),
                        reads=[bb["KF"], b_w02t], writes=[bank[i]])
                    S.c("tensor", lambda e, sc=sc, pk=pk: e.matmul(
                        pk[:, 256:384], VF3[:, sc, :], w1n_sb3[:, h, :],
                        start=True, stop=True),
                        reads=[bb["VF"], b_w1n], writes=[bank[i]])
                span = pgrp[:, 0:3 * 512].rearrange(
                    "p (b c) -> p b c", b=3)[:, 0:ns, 0:384]
                S.c("scalar", lambda e, w0sc=w0sc, ns=ns, span=span: e.activation(
                    GH3[:, w0sc:w0sc + ns, :], span, AF.Identity),
                    reads=[], writes=[bank[0], bank[1], bank[2], bb["GH"]])
                if dwg is not None:
                    next(dwg, None)

            S.c("vector", lambda e: e.tensor_mul(
                gbahbm, gbahbm,
                rk3[:, u0:u0 + 16, h:h + 1].broadcast_to([128, 16, 256])),
                reads=[b_rk], writes=[bb["GH"]])
            S.c("scalar", lambda e: e.activation(DS3, gba, AF.Sigmoid),
                reads=[bb["GH"]], writes=[bb["DS"]])
            S.c("vector", lambda e: e.tensor_mul(SG3, gba, DS3),
                reads=[bb["GH"], bb["DS"]], writes=[bb["SG"]])
            S.c("vector", lambda e: e.tensor_mul(HID3, SG3, hbm),
                reads=[bb["SG"], bb["GH"]], writes=[bb["HID"]])
            S.c("vector", lambda e: e.tensor_mul(DGA3, DH3, hbm),
                reads=[bb["DH"], bb["GH"]], writes=[bb["DGA"]])
            S.c("gpsimd", lambda e: e.tensor_mul(DB23, DH3, SG3),
                reads=[bb["DH"], bb["SG"]], writes=[bb["DB2"]])
            S.c("vector", lambda e: e.tensor_tensor(gba, gba, SG3, ALU.subtract),
                reads=[bb["SG"]], writes=[bb["GH"]])
            S.c("vector", lambda e: e.scalar_tensor_tensor(
                DS3, gba, 1.0, DS3, ALU.add, ALU.mult),
                reads=[bb["GH"]], writes=[bb["DS"]])
            S.c("vector", lambda e: e.tensor_mul(sl["DGA"], sl["DGA"], sl["DS"]),
                reads=[bb["DS"]], writes=[bb["DGA"]])
            lrk3 = sm["lrk"].rearrange("p (a c) -> p a c", a=2)
            S.c("vector", lambda e: e.tensor_mul(
                lrk3[:, 0, :], lr3[:, u0:u0 + 16, 3 * h], rk3[:, u0:u0 + 16, h]),
                reads=[b_lr, b_rk], writes=[bb["lrk"]])
            S.c("vector", lambda e: e.tensor_mul(
                lrk3[:, 1, :], lr3[:, u0:u0 + 16, 3 * h + 2], rk3[:, u0:u0 + 16, h]),
                reads=[b_lr, b_rk], writes=[bb["lrk"]])
            S.c("vector", lambda e: e.tensor_mul(
                KF3, KT3, lrk3[:, 1, :][:, :, None].broadcast_to([128, 16, 128])),
                reads=[bb["KT"], bb["lrk"]], writes=[bb["KF"]])
            S.c("vector", lambda e: e.tensor_mul(
                KT3, KT3, lrk3[:, 0, :][:, :, None].broadcast_to([128, 16, 128])),
                reads=[bb["lrk"]], writes=[bb["KT"]])
            S.c("vector", lambda e: e.tensor_mul(
                HID3, HID3,
                lr3[:, u0:u0 + 16, 3 * h + 1][:, :, None].broadcast_to([128, 16, 128])),
                reads=[b_lr], writes=[bb["HID"]])

        def p4a_dw_gen(hm):
            p = hm % 2
            r = hm % 3
            sl, sm, bb = par[p]
            bb = dict(bb)
            bb["KF"] = b_kfr[r]
            bank = [b_pb[3 * p + i] for i in range(3)]
            pbk = [pb[3 * p + i] for i in range(3)]
            KF3 = kfr[r].rearrange("p (a c) -> p a c", a=16)
            KT3 = sl["KT"].rearrange("p (a c) -> p a c", a=16)
            VT3 = sl["VT"].rearrange("p (a c) -> p a c", a=16)
            HID3 = sl["HID"].rearrange("p (a c) -> p a c", a=16)
            DGA3 = sl["DGA"].rearrange("p (a c) -> p a c", a=16)
            DB23 = sl["DB2"].rearrange("p (a c) -> p a c", a=16)
            ops = ((DGA3, KT3), (VT3, HID3), (DB23, KF3))
            nmA = ("DGA", "VT", "DB2")
            nmB = ("KT", "HID", "KF")
            for j in range(3):
                A3, B3 = ops[j]
                for sc in range(16):
                    S.c("tensor", lambda e, sc=sc, j=j, A3=A3, B3=B3, pbk=pbk:
                        e.matmul(pbk[j][:, 0:128], A3[:, sc, :], B3[:, sc, :],
                                 start=(sc == 0), stop=(sc == 15)),
                        reads=[bb[nmA[j]], bb[nmB[j]]], writes=[bank[j]])
                yield None
                for sc in range(16):
                    S.c("tensor", lambda e, sc=sc, j=j, A3=A3, B3=B3, pbk=pbk:
                        e.matmul(pbk[j][:, 128:256], B3[:, sc, :], A3[:, sc, :],
                                 start=(sc == 0), stop=(sc == 15)),
                        reads=[bb[nmA[j]], bb[nmB[j]]], writes=[bank[j]])
                idx = hm * 3 + j
                S.c("vector", lambda e, j=j, idx=idx, pbk=pbk: e.tensor_copy(
                    DWS3[:, idx, :], pbk[j][:, 0:128]),
                    reads=[], writes=[bank[j], b_dw[idx]])
                S.c("scalar", lambda e, j=j, idx=idx, pbk=pbk: e.activation(
                    XTS3[:, idx, :], pbk[j][:, 128:256], AF.Identity),
                    reads=[], writes=[bank[j], b_xt96[idx]])
                S.c("vector", lambda e, j=j, idx=idx: e.scalar_tensor_tensor(
                    sl["SG"][:, 0:128], DWS3[:, idx, :], 1.0, DWS3[:, idx, :],
                    ALU.mult, ALU.mult, accum_out=cs96[:, idx:idx + 1]),
                    reads=[b_dw[idx]], writes=[b_cs, bb["SG"]])
                yield None

        def p4a_dw(hm):
            for _ in p4a_dw_gen(hm):
                pass

        p4a_streams(0)
        p4a_streams(1)
        for hm in range(P4_HM):
            dwg = p4a_dw_gen(hm - 1) if hm > 0 else None
            p4a(hm, dwg)
            if dwg is not None:
                for _ in dwg:
                    pass
            if hm + 2 < P4_HM:
                p4a_streams(hm + 2)
        p4a_dw(P4_HM - 1)

        fence_b = tuple(S.cut())
        ar1.cur = ar1_mark
        hb_t = ar1.take(NI * 512)
        hb3 = hb_t.rearrange("p (a t) -> p a t", a=NI)
        b_hb = Buf(fence_b)
        wdb = [ar1.take(NI * 128), ar1.take(NI * 128)]
        wdb3 = [t.rearrange("p (a c) -> p a c", a=NI) for t in wdb]
        b_wdb = [Buf(fence_b), Buf(fence_b)]
        ffs = [ar2.take(512), ar2.take(512)]
        b_ffs = [ar2.buf(), ar2.buf()]

        def p6_units():
            for tb in range(8):
                half = tb // 4
                for seg in range(4):
                    S.d("sync", lambda e, tb=tb, seg=seg: e.dma_start(
                        hb3[:, seg * 11:(seg + 1) * 11, :],
                        hdn3[:, seg * 11:(seg + 1) * 11,
                             tb * 512:(tb + 1) * 512]),
                        reads=[b_hdn_h[half]], writes=[b_hb])
                for oc in range(16):
                    ws = oc % 2
                    S.d("gpsimd", lambda e, oc=oc, ws=ws: e.dma_start(
                        wdb3[ws], wdbf3[:, :, oc * 128:(oc + 1) * 128]),
                        reads=[b_wdbf], writes=[b_wdb[ws]])
                    for a in range(NI):
                        S.c("tensor", lambda e, a=a, ws=ws: e.matmul(
                            pb[6][:, 0:512], wdb3[ws][:, a, :],
                            hb3[:, a, :],
                            start=(a == 0), stop=(a == NI - 1)),
                            reads=[b_wdb[ws], b_hb], writes=[b_pb[6]])
                    fs = oc % 2
                    S.c("vector", lambda e, oc=oc, fs=fs:
                        e.tensor_scalar(
                            ffs[fs], pb[6][:, 0:512],
                            down_s_sb[:, oc:oc + 1], None, ALU.mult),
                        reads=[b_scales], writes=[b_pb[6], b_ffs[fs]])
                    tg = tb * 512
                    S.d("sync", lambda e, oc=oc, tg=tg, fs=fs: e.dma_start(
                        part_h[oc * 128:(oc + 1) * 128, tg:tg + 512], ffs[fs]),
                        reads=[b_ffs[fs]], writes=[b_part])
                    yield None

        p6_gen = p6_units() if STAGE >= 5 else iter(())

        def p6_feed(n):
            for _ in range(n):
                try:
                    next(p6_gen)
                except StopIteration:
                    return

        nsb_t = [ar1.take(3 * 128) for _ in range(2)]
        nsB_t = [ar1.take(3 * 128) for _ in range(2)]
        nsb3 = [t.rearrange("p (b c) -> p b c", b=3) for t in nsb_t]
        nsB3 = [t.rearrange("p (b c) -> p b c", b=3) for t in nsB_t]
        b_nsb = [Buf(fence_b), Buf(fence_b)]
        b_nsB = [Buf(fence_b), Buf(fence_b)]
        S.c("vector", lambda e: e.tensor_copy(csb96[:, 0:96], cs96),
            reads=[b_cs], writes=[b_csb])
        S.c("tensor", lambda e: e.matmul(
            pb[6][0:1, 0:96], onescol[:, :], csb96[:, 0:96],
            start=True, stop=True),
            reads=[b_csb, b_onescol], writes=[b_pb[6]])
        S.c("scalar", lambda e: e.activation(
            scl96[0:1, :], pb[6][0:1, 0:96], AF.Sqrt),
            reads=[], writes=[b_pb[6], b_scl])
        S.c("vector", lambda e: e.tensor_scalar(
            scl96[0:1, :], scl96[0:1, :], float(1e-7), None, ALU.add),
            reads=[], writes=[b_scl])
        S.c("vector", lambda e: e.reciprocal(scl96[0:1, :], scl96[0:1, :]),
            reads=[], writes=[b_scl])
        S.c("vector", lambda e: e.tensor_copy(csb96[0:1, 0:96], scl96[0:1, :]),
            reads=[b_scl], writes=[b_csb])
        S.c("tensor", lambda e: e.matmul(
            pb[6][:, 128:224], onesrow[:, :], csb96[0:1, 0:96],
            start=True, stop=True),
            reads=[b_csb, b_onesrow], writes=[b_pb[6]])
        S.c("vector", lambda e: e.tensor_copy(scl96[:, :], pb[6][:, 128:224]),
            reads=[], writes=[b_pb[6], b_scl])
        S.c("vector", lambda e: e.tensor_mul(
            DWS3[:, 0:NMAT, :], DWS3[:, 0:NMAT, :],
            scl96[:, 0:NMAT][:, :, None].broadcast_to([128, NMAT, 128])),
            reads=[b_scl] + b_dw[:NMAT], writes=b_dw[:NMAT])
        S.c("vector", lambda e: e.tensor_mul(
            XTS3[:, 0:NMAT, :], XTS3[:, 0:NMAT, :],
            scl96[:, 0:NMAT][:, :, None].broadcast_to([128, NMAT, 128])),
            reads=[b_scl] + b_xt96[:NMAT], writes=b_xt96[:NMAT])
        for k, (a_k, b_k, c_k) in enumerate(NS_COEFS):
            cb2 = float(c_k / (b_k * b_k))
            for g0 in range(0, NMAT, 6):
                p6_feed(1)
                grp = list(range(g0, min(g0 + 6, NMAT)))
                halves = [[i for i in grp if i % 6 < 3],
                          [i for i in grp if i % 6 >= 3]]
                def bankof(idx):
                    return b_pb[idx % 6], pb[idx % 6]
                for idx in grp:
                    bk, pk = bankof(idx)
                    xtap = XTS3[:, idx, :]
                    S.c("tensor", lambda e, pk=pk, xtap=xtap: e.matmul(
                        pk[:, 0:128], xtap, xtap, start=True, stop=True),
                        reads=[b_xt96[idx]], writes=[bk])
                for hf in range(2):
                    ids = halves[hf]
                    if not ids:
                        continue
                    n = len(ids)
                    span = pbig[hf][:, 0:1536].rearrange(
                        "p (b c) -> p b c", b=3)[:, 0:n, 0:128]
                    S.c("vector", lambda e, hf=hf, n=n, span=span, b_k=b_k:
                        e.tensor_scalar(nsb3[hf][:, 0:n, :], span, float(b_k),
                                        None, ALU.mult),
                        reads=[], writes=[b_pb[3 * hf], b_pb[3 * hf + 1],
                                          b_pb[3 * hf + 2], b_nsb[hf]])
                for idx in grp:
                    bk, pk = bankof(idx)
                    hf = 0 if idx % 6 < 3 else 1
                    bq = idx % 3
                    S.c("tensor", lambda e, pk=pk, hf=hf, bq=bq: e.matmul(
                        pk[:, 128:256], nsb3[hf][:, bq, :], nsb3[hf][:, bq, :],
                        start=True, stop=True),
                        reads=[b_nsb[hf]], writes=[bk])
                for hf in range(2):
                    ids = halves[hf]
                    if not ids:
                        continue
                    n = len(ids)
                    span = pbig[hf][:, 0:1536].rearrange(
                        "p (b c) -> p b c", b=3)[:, 0:n, 128:256]
                    S.c("vector", lambda e, hf=hf, n=n, span=span, cb2=cb2:
                        e.scalar_tensor_tensor(
                            nsB3[hf][:, 0:n, :], span, cb2, nsb3[hf][:, 0:n, :],
                            ALU.mult, ALU.add),
                        reads=[b_nsb[hf]], writes=[b_pb[3 * hf], b_pb[3 * hf + 1],
                                                   b_pb[3 * hf + 2], b_nsB[hf]])
                for idx in grp:
                    bk, pk = bankof(idx)
                    hf = 0 if idx % 6 < 3 else 1
                    bq = idx % 3
                    xap = DWS3[:, idx, :]
                    S.c("tensor", lambda e, pk=pk, hf=hf, bq=bq, xap=xap: e.matmul(
                        pk[:, 256:384], nsB3[hf][:, bq, :], xap,
                        start=True, stop=False),
                        reads=[b_nsB[hf], b_dw[idx]], writes=[bk])
                    S.c("tensor", lambda e, pk=pk, k=k, xap=xap: e.matmul(
                        pk[:, 256:384], aI[:, k * 128:(k + 1) * 128], xap,
                        start=False, stop=True),
                        reads=[b_aI, b_dw[idx]], writes=[bk])
                for idx in grp:
                    bk, pk = bankof(idx)
                    hf = 0 if idx % 6 < 3 else 1
                    bq = idx % 3
                    xap = DWS3[:, idx, :]
                    S.c("tensor", lambda e, pk=pk, hf=hf, bq=bq, xap=xap: e.matmul(
                        pk[:, 384:512], xap, nsB3[hf][:, bq, :],
                        start=True, stop=False),
                        reads=[b_nsB[hf], b_dw[idx]], writes=[bk])
                    S.c("tensor", lambda e, pk=pk, k=k, xap=xap: e.matmul(
                        pk[:, 384:512], xap, aI[:, k * 128:(k + 1) * 128],
                        start=False, stop=True),
                        reads=[b_aI, b_dw[idx]], writes=[bk])
                for hf in range(2):
                    ids = halves[hf]
                    if not ids:
                        continue
                    n = len(ids)
                    lo = ids[0]
                    span = pbig[hf][:, 0:1536].rearrange(
                        "p (b c) -> p b c", b=3)[:, 0:n, 256:384]
                    S.c("scalar", lambda e, span=span, lo=lo, n=n: e.activation(
                        DWS3[:, lo:lo + n, :], span, AF.Identity),
                        reads=[], writes=[b_pb[3 * hf], b_pb[3 * hf + 1],
                                          b_pb[3 * hf + 2]] + b_dw[lo:lo + n])
                    span2 = pbig[hf][:, 0:1536].rearrange(
                        "p (b c) -> p b c", b=3)[:, 0:n, 384:512]
                    S.c("vector", lambda e, span2=span2, lo=lo, n=n: e.tensor_copy(
                        XTS3[:, lo:lo + n, :], span2),
                        reads=[], writes=[b_pb[3 * hf], b_pb[3 * hf + 1],
                                          b_pb[3 * hf + 2]] + b_xt96[lo:lo + n])

        fence_c = tuple(S.cut())
        parc = []
        for p in range(2):
            ar1.cur = xts_off if p == 0 else 60160
            sl = {}
            for name in ("QF", "GHS", "SIL", "OS", "OSQ"):
                sl[name] = ar1.take(2048)
            sl["WT"] = ar1.take(3 * 128)
            sm = {"WN": ar2.take(3 * 128), "WPD": ar2.take(3 * 128),
                  "rc": ar2.take(16), "oss": ar2.take(64)}
            bufs = {}
            for k in list(sl) + list(sm):
                bufs[k] = Buf(fence_c)
            parc.append((sl, sm, bufs))

        def p4c_pre(hm):
            half = hm // H
            h = hm % H
            p = hm % 2
            sl, sm, bb = parc[p]
            QF = sl["QF"]
            WT3 = sl["WT"].rearrange("p (a c) -> p a c", a=3)
            WN3 = sm["WN"].rearrange("p (a c) -> p a c", a=3)
            WPD3 = sm["WPD"].rearrange("p (a c) -> p a c", a=3)
            dsl = DWS3[:, hm * 3:(hm + 1) * 3, :]
            dbufs = [b_dw[hm * 3 + j] for j in range(3)]
            hh = half * 16 + h
            S.d("sync", lambda e, hh=hh: e.dma_start(
                QF, qT_h[hh], transpose=True),
                reads=[b_q_h[half]], writes=[bb["QF"]])
            S.d("sync", lambda e: e.dma_start(WN3, wnat3[:, :, h, :]),
                writes=[bb["WN"]])
            S.c("vector", lambda e: e.tensor_tensor(WPD3, WN3, dsl, ALU.add),
                reads=[bb["WN"]] + dbufs, writes=[bb["WPD"]])
            for j in range(3):
                S.c("vector", lambda e, j=j: e.scalar_tensor_tensor(
                    sl["OSQ"][:, 0:128], WPD3[:, j, :], 1.0, WPD3[:, j, :],
                    ALU.mult, ALU.mult, accum_out=sm["rc"][:, j:j + 1]),
                    reads=[bb["WPD"]], writes=[bb["rc"], bb["OSQ"]])
            S.c("scalar", lambda e: e.activation(
                sm["rc"][:, 4:7], sm["rc"][:, 0:3], AF.Sqrt),
                reads=[bb["rc"]], writes=[bb["rc"]])
            S.c("vector", lambda e: e.tensor_scalar(
                sm["rc"][:, 4:7], sm["rc"][:, 4:7], float(1e-5), None, ALU.add),
                reads=[], writes=[bb["rc"]])
            S.c("vector", lambda e: e.reciprocal(
                sm["rc"][:, 8:11], sm["rc"][:, 4:7]),
                reads=[], writes=[bb["rc"]])
            S.c("vector", lambda e, h=h: e.tensor_mul(
                sm["rc"][:, 8:11], sm["rc"][:, 8:11], wnorm_sb[:, h:48:16]),
                reads=[b_wnorm], writes=[bb["rc"]])
            S.c("vector", lambda e: e.tensor_mul(
                WT3, WPD3,
                sm["rc"][:, 8:11][:, :, None].broadcast_to([128, 3, 128])),
                reads=[bb["WPD"], bb["rc"]], writes=[bb["WT"]])
            for j in range(3):
                S.c("tensor", lambda e, j=j: e.transpose(
                    pbt[:, 0:128], WT3[:, j, :], ident[:, :]),
                    reads=[bb["WT"], b_ident], writes=[b_pbt])
                S.c("vector", lambda e, j=j: e.tensor_copy(
                    WT3[:, j, :], pbt[:, 0:128]),
                    reads=[], writes=[b_pbt, bb["WT"]])

        def p4c_apply(hm):
            half = hm // H
            h = hm % H
            p = hm % 2
            sl, sm, bb = parc[p]
            bank = [b_pb[3 * p + i] for i in range(2)]
            pbk = [pb[3 * p + i] for i in range(2)]
            u0 = half * 16
            QF = sl["QF"]
            GHS = sl["GHS"]
            SIL = sl["SIL"]
            OS = sl["OS"]
            OS3 = sl["OS"].rearrange("p (a c) -> p a c", a=16)
            OSQ3 = sl["OSQ"].rearrange("p (a c) -> p a c", a=16)
            WT3 = sl["WT"].rearrange("p (a c) -> p a c", a=3)
            for t in range(4):
                S.c("tensor", lambda e, t=t: e.matmul(
                    pbk[0][:, 0:512], WT3[:, 0, :], QF[:, t * 512:(t + 1) * 512],
                    start=True, stop=True),
                    reads=[bb["WT"], bb["QF"]], writes=[bank[0]])
                S.c("tensor", lambda e, t=t: e.matmul(
                    pbk[1][:, 0:512], WT3[:, 2, :], QF[:, t * 512:(t + 1) * 512],
                    start=True, stop=True),
                    reads=[bb["WT"], bb["QF"]], writes=[bank[1]])
                S.c("scalar", lambda e, t=t: e.activation(
                    SIL[:, t * 512:(t + 1) * 512], pbk[0][:, 0:512], AF.Silu),
                    reads=[], writes=[bank[0], bb["SIL"]])
                S.c("vector", lambda e, t=t: e.scalar_tensor_tensor(
                    GHS[:, t * 512:(t + 1) * 512], pbk[1][:, 0:512], 1.0,
                    SIL[:, t * 512:(t + 1) * 512], ALU.bypass, ALU.mult),
                    reads=[bb["SIL"]], writes=[bank[1], bb["GHS"]])
            for scg in range(4):
                bk = bank[scg % 2]
                pk = pbk[scg % 2]
                for si in range(4):
                    sc = scg * 4 + si
                    S.c("tensor", lambda e, sc=sc, si=si, pk=pk: e.matmul(
                        pk[:, si * 128:(si + 1) * 128],
                        GHS[:, sc * 128:(sc + 1) * 128], WT3[:, 1, :],
                        start=True, stop=True),
                        reads=[bb["GHS"], bb["WT"]], writes=[bk])
                S.c("vector", lambda e, scg=scg, pk=pk: e.tensor_copy(
                    OS[:, scg * 512:(scg + 1) * 512], pk[:, 0:512]),
                    reads=[], writes=[bk, bb["OS"]])
            oss3 = sm["oss"].rearrange("p (a c) -> p a c", a=4)
            S.c("vector", lambda e: e.tensor_mul(OSQ3, OS3, OS3),
                reads=[bb["OS"]], writes=[bb["OSQ"]])
            S.c("vector", lambda e: e.tensor_reduce(
                oss3[:, 0, :], OSQ3, AX.X, ALU.add),
                reads=[bb["OSQ"]], writes=[bb["oss"]])
            S.c("scalar", lambda e: e.activation(
                oss3[:, 1, :], oss3[:, 0, :], AF.Sqrt,
                scale=f32(1.0 / 128.0), bias=f32(1e-5)),
                reads=[], writes=[bb["oss"]])
            S.c("vector", lambda e: e.reciprocal(
                oss3[:, 2, :], oss3[:, 1, :]),
                reads=[], writes=[bb["oss"]])
            S.c("vector", lambda e: e.tensor_mul(
                OS3, OS3, oss3[:, 2, :][:, :, None].broadcast_to([128, 16, 128])),
                reads=[bb["oss"]], writes=[bb["OS"]])
            S.c("vector", lambda e: e.tensor_mul(
                OS3, OS3, gam[:, None, :].broadcast_to([128, 16, 128])),
                reads=[b_gam], writes=[bb["OS"]])
            S.d("sync", lambda e, half=half, h=h: e.dma_start(
                onrm4[:, half * 16 + h, :, :], OS3),
                reads=[bb["OS"]], writes=[b_onrm_h[half]])

        for hm in range(P4_HM):
            p4c_pre(hm)
            if hm > 0:
                p4c_apply(hm - 1)
            p6_feed(2)
        p4c_apply(P4_HM - 1)
        p6_feed(128)
        ar1.reset()
        ar2.reset()

    if STAGE >= 5:
        of_t = ar1.take(16 * 2048)
        of3 = of_t.rearrange("p (h t) -> p h t", h=16)
        b_of = ar1.buf()
        wo_buf = [ar1.take(16 * 128), ar1.take(16 * 128)]
        wo_b3 = [t.rearrange("p (a c) -> p a c", a=16) for t in wo_buf]
        b_wo = [ar1.buf(), ar1.buf()]
        xt_buf = [ar2.take(512) for _ in range(4)]
        b_xt = [ar2.buf() for _ in range(4)]
        ft_buf = [ar2.take(512) for _ in range(4)]
        b_ft = [ar2.buf() for _ in range(4)]
        po_buf = [ar2.take(512) for _ in range(4)]
        b_po = [ar2.buf() for _ in range(4)]
        for half in range(2):
            t0 = half * 2048
            for hh in range(16):
                S.d("sync", lambda e, hh=hh, half=half: e.dma_start(
                    of3[:, hh, :], onrm_h[half * 16 + hh], transpose=True),
                    reads=[b_onrm_h[half]], writes=[b_of])
            for oc in range(16):
                ws = oc % 2
                S.d("gpsimd", lambda e, oc=oc, ws=ws: e.dma_start(
                    wo_b3[ws], wo3[:, :, oc * 128:(oc + 1) * 128]),
                    writes=[b_wo[ws]])
                for t in range(4):
                    for hh in range(16):
                        S.c("tensor", lambda e, hh=hh, t=t, ws=ws: e.matmul(
                            pb[t][:, 0:512], wo_b3[ws][:, hh, :],
                            of3[:, hh, t * 512:(t + 1) * 512],
                            start=(hh == 0), stop=(hh == 15)),
                            reads=[b_wo[ws], b_of], writes=[b_pb[t]])
                    S.d("sync", lambda e, oc=oc, t=t, t0=t0: e.dma_start(
                        xt_buf[t], x_feat[oc * 128:(oc + 1) * 128,
                                          t0 + t * 512: t0 + (t + 1) * 512]),
                        writes=[b_xt[t]])
                    S.d("sync", lambda e, oc=oc, t=t, t0=t0: e.dma_start(
                        ft_buf[t], part_h[oc * 128:(oc + 1) * 128,
                                          t0 + t * 512: t0 + (t + 1) * 512]),
                        reads=[b_part], writes=[b_ft[t]])
                    S.c("vector", lambda e, t=t: e.scalar_tensor_tensor(
                        po_buf[t], pb[t][:, 0:512], 1.0, xt_buf[t],
                        ALU.bypass, ALU.add),
                        reads=[b_xt[t]], writes=[b_pb[t], b_po[t]])
                    S.c("vector", lambda e, t=t: e.tensor_add(
                        po_buf[t], po_buf[t], ft_buf[t]),
                        reads=[b_ft[t]], writes=[b_po[t]])
                    oid = S.d("sync", lambda e, oc=oc, t=t, t0=t0: e.dma_start(
                        out_feat[oc * 128:(oc + 1) * 128,
                                 t0 + t * 512: t0 + (t + 1) * 512], po_buf[t]),
                        reads=[b_po[t]], writes=[b_out])
                    out_dmas.append(oid)
        ar1.reset()
        ar2.reset()

    S.emit(sorted(set(out_dmas) | set(S.cut())))
    return nc


_NC_CACHE = None


def _get_nc():
    global _NC_CACHE
    if _NC_CACHE is None:
        _NC_CACHE = build_nc()
    return _NC_CACHE


def _prep_shared(ins):
    w0 = np.asarray(ins["L0_attn_w0"])
    w1 = np.asarray(ins["L0_attn_w1"])
    w2 = np.asarray(ins["L0_attn_w2"])
    return {
        "wqkv_t": np.ascontiguousarray(
            np.concatenate([np.asarray(ins["L0_attn_to_qkv_w"]),
                            np.asarray(ins["L0_attn_lr_proj_w"])], 0).T),
        "wg_t": np.ascontiguousarray(np.asarray(ins["L0_ffn_gate_w"]).T),
        "wu_t": np.ascontiguousarray(np.asarray(ins["L0_ffn_up_w"]).T),
        "wd_t": np.ascontiguousarray(np.asarray(ins["L0_ffn_down_w"]).T),
        "wo_t": np.ascontiguousarray(np.asarray(ins["L0_attn_o_proj_w"]).T),
        "w_nat": np.ascontiguousarray(
            np.stack([w0, w1, w2], 0).transpose(2, 0, 1, 3)
            .reshape(128, 3 * H * 128)),
        "w02t": np.ascontiguousarray(
            np.concatenate([w0.transpose(0, 2, 1), w2.transpose(0, 2, 1)], 2)
            .transpose(1, 0, 2).reshape(128, H * 256)),
        "gate_s_r": np.ascontiguousarray(
            np.asarray(ins["L0_ffn_gate_s"]).reshape(NI, 128).T),
        "up_s_r": np.ascontiguousarray(
            np.asarray(ins["L0_ffn_up_s"]).reshape(NI, 128).T),
        "down_s_r": np.ascontiguousarray(
            np.asarray(ins["L0_ffn_down_s"]).reshape(16, 128).T),
        "onorm_r": np.ascontiguousarray(
            np.asarray(ins["L0_attn_o_norm_w"]).reshape(1, 128)),
    }


def kernel(**inputs):
    x = np.asarray(inputs["L0_attn_x"], dtype=np.float32)
    shared = _prep_shared(inputs)
    in_maps = []
    for c in range(NCORES):
        xc = x[2 * c:2 * c + 2]
        m = dict(shared)
        m["x_tok"] = np.ascontiguousarray(xc.reshape(T, D))
        m["x_feat"] = np.ascontiguousarray(
            np.concatenate([xc[0].T, xc[1].T], axis=1))
        in_maps.append(m)
    nc = _get_nc()
    res = run_bass_kernel_spmd(nc, in_maps, core_ids=list(range(NCORES)))
    out = np.empty((16, SEQ, D), dtype=np.float32)
    for c in range(NCORES):
        of = res.results[c]["out_feat"]
        out[2 * c] = of[:, 0:SEQ].T
        out[2 * c + 1] = of[:, SEQ:2 * SEQ].T
    return out
